# version 03
# baseline (speedup 1.0000x reference)
import numpy as np
import ml_dtypes
import concourse.bass as bass
import concourse.mybir as mybir
from concourse.bass_utils import run_bass_kernel_spmd

F32 = mybir.dt.float32
BF16 = mybir.dt.bfloat16
Alu = mybir.AluOpType
Act = mybir.ActivationFunctionType
AX = mybir.AxisListType

D = 1024
L = 4096
LQ = 2048
H = 16
FI = 2816
ICH = 256
NIC = FI // ICH
VPW = 132


class Buf:
    __slots__ = ("ready", "users")
    def __init__(self):
        self.ready = []
        self.users = []


class Prog:
    ENGINES = ("sync", "gpsimd", "vector", "scalar", "tensor")
    DMA_ENG = {"sync", "gpsimd"}

    def __init__(self, nc):
        self.nc = nc
        self.ops = []
        self.count = {e: 0 for e in self.ENGINES}
        self.B = {}

    def buf(self, key):
        if key not in self.B:
            self.B[key] = Buf()
        return self.B[key]

    def alias_seed(self, new_keys, old_keys):
        toks = []
        for ok in old_keys:
            b = self.buf(ok)
            toks += b.ready + b.users
        for nk in new_keys:
            self.buf(nk).users.extend(toks)

    def emit(self, eng, meth, *args, r=(), w=(), **kw):
        inc = 16 if eng in self.DMA_ENG else 1
        if SERIAL:
            waits = {e: c for e, c in self.count.items() if c > 0 and e != eng}
            self.count[eng] += inc
            tok = (eng, self.count[eng])
            for b in [self.buf(k) for k in w]:
                b.ready = [tok]; b.users = []
            for b in [self.buf(k) for k in r]:
                b.users.append(tok)
            self.ops.append((eng, meth, args, kw, tuple(waits.items()), inc))
            return tok
        waits = {}
        def add(tok, raw=False):
            e, v = tok
            if e != eng:
                waits[e] = max(waits.get(e, 0), v)
            elif raw and eng == "vector" and (self.count[eng] + inc - v) <= 1:
                waits[e] = max(waits.get(e, 0), v)
        reads = [self.buf(k) for k in r]
        writes = [self.buf(k) for k in w]
        for b in reads:
            for t in b.ready:
                add(t, raw=True)
        for b in writes:
            for t in b.ready:
                add(t)
            for t in b.users:
                add(t)
        self.count[eng] += inc
        tok = (eng, self.count[eng])
        for b in writes:
            b.ready = [tok]
            b.users = []
        for b in reads:
            b.users.append(tok)
        self.ops.append((eng, meth, args, kw, tuple(waits.items()), inc))
        return tok

    def finalize(self):
        nc = self.nc
        final = [(e, self.count[e]) for e in self.ENGINES if self.count[e] > 0]
        from contextlib import ExitStack
        stack = ExitStack()
        sems = {e: stack.enter_context(nc.semaphore(f"s_{e}")) for e in self.ENGINES}
        block = stack.enter_context(nc.Block())
        ops = self.ops

        def emit_for(name):
            def emitf(eng):
                for (enm, meth, args, kw, waits, inc) in ops:
                    if enm != name:
                        continue
                    for (we, wv) in waits:
                        eng.wait_ge(sems[we], wv)
                    getattr(eng, meth)(*args, **kw).then_inc(sems[enm], inc)
                for (we, wv) in final:
                    if we != name or name in Prog.DMA_ENG:
                        eng.wait_ge(sems[we], wv)
            return emitf
        block.sync(emit_for("sync"))
        block.gpsimd(emit_for("gpsimd"))
        block.vector(emit_for("vector"))
        block.scalar(emit_for("scalar"))
        block.tensor(emit_for("tensor"))
        stack.close()


DEBUG = False
SERIAL = False

def build_program():
    nc = bass.Bass("TRN2", target_bir_lowering=False)
    P = Prog(nc)
    E = P.emit

    def din(name, shape, dt=F32):
        return nc.dram_tensor(name, shape, dt, kind="ExternalInput").ap()
    x_all = din("x_all", [L, D])
    x_own = din("x_own", [LQ, D])
    w_qkv = din("w_qkv", [3 * D, D])
    w_out = din("w_out", [D, D])
    w_gate = din("w_gate", [FI, D])
    w_up = din("w_up", [FI, D])
    w_down = din("w_down", [D, FI])
    cos_f = din("cos_f", [128, L])
    sin_f = din("sin_f", [128, L])
    cos_o = din("cos_o", [128, LQ])
    sin_o = din("sin_o", [128, LQ])
    ident = din("ident", [128, 128], BF16)
    out = nc.dram_tensor("out", [LQ, D], F32, kind="ExternalOutput").ap()

    scrkind = "ExternalOutput" if DEBUG else "Internal"
    kscr = nc.dram_tensor("kscr", [8, 128, L], BF16, kind=scrkind).ap()
    qscr = nc.dram_tensor("qscr", [H, 64, LQ], BF16, kind=scrkind).ap()
    mscr = nc.dram_tensor("mscr", [H, LQ], BF16, kind=scrkind).ap()
    vscr = nc.dram_tensor("vscr", [8, 32, 128, VPW], BF16, kind=scrkind).ap()
    wgscr = nc.dram_tensor("wgscr", [8, 128, FI], BF16, kind=scrkind).ap()
    wuscr = nc.dram_tensor("wuscr", [8, 128, FI], BF16, kind=scrkind).ap()
    wdscr = nc.dram_tensor("wdscr", [FI // 128, 128, D], BF16, kind=scrkind).ap()
    if DEBUG:
        dbg_xlnf = nc.dram_tensor("dbg_xlnf", [128, 4096], BF16, kind="ExternalOutput").ap()
        dbg_xs = nc.dram_tensor("dbg_xs", [128, D], F32, kind="ExternalOutput").ap()
        dbg_xc = nc.dram_tensor("dbg_xc", [128, D], F32, kind="ExternalOutput").ap()
        dbg_xlnt = nc.dram_tensor("dbg_xlnt", [128, D], BF16, kind="ExternalOutput").ap()
        dbg_id = nc.dram_tensor("dbg_id", [128, 128], BF16, kind="ExternalOutput").ap()
        dbg_lnv = nc.dram_tensor("dbg_lnv", [128, 2], F32, kind="ExternalOutput").ap()
        dbg_kpre = nc.dram_tensor("dbg_kpre", [2, 128, 512], F32, kind="ExternalOutput").ap()
        dbg_q = nc.dram_tensor("dbg_q", [8, 128, LQ], BF16, kind="ExternalOutput").ap()
        dbg_kp = nc.dram_tensor("dbg_kp", [128, L], BF16, kind="ExternalOutput").ap()
        dbg_vp = nc.dram_tensor("dbg_vp", [128, 32 * VPW], BF16, kind="ExternalOutput").ap()
        dbg_pt = nc.dram_tensor("dbg_pt", [128, 2048], BF16, kind="ExternalOutput").ap()
        dbg_av = nc.dram_tensor("dbg_av", [64, 512], F32, kind="ExternalOutput").ap()
        dbg_avd = nc.dram_tensor("dbg_avd", [1, 512], F32, kind="ExternalOutput").ap()
        dbg_st = nc.dram_tensor("dbg_st", [128, 2048], F32, kind="ExternalOutput").ap()
        dbg_hb = nc.dram_tensor("dbg_hb", [128, 16 * ICH], F32, kind="ExternalOutput").ap()
        dbg_hq = nc.dram_tensor("dbg_hq", [128, 16 * ICH], BF16, kind="ExternalOutput").ap()
        dbg_wg = nc.dram_tensor("dbg_wg", [128, ICH], BF16, kind="ExternalOutput").ap()
        dbg_gt = nc.dram_tensor("dbg_gt", [128, ICH], F32, kind="ExternalOutput").ap()
        dbg_y15 = nc.dram_tensor("dbg_y15", [4, 128, D], F32, kind="ExternalOutput").ap()
        dbg_hqt = nc.dram_tensor("dbg_hqt", [128, LQ], BF16, kind="ExternalOutput").ap()
        dbg_attn = nc.dram_tensor("dbg_attn", [8, 128, LQ], BF16, kind="ExternalOutput").ap()
        dbg_bias = nc.dram_tensor("dbg_bias", [1, H], F32, kind="ExternalOutput").ap()
        dbg_y1 = nc.dram_tensor("dbg_y1", [16, 128, D], F32, kind="ExternalOutput").ap()
        dbg_xq = nc.dram_tensor("dbg_xq", [8, 128, LQ], BF16, kind="ExternalOutput").ap()

    def sb(name, shape, dt):
        return nc.alloc_sbuf_tensor(name, shape, dt)

    s_id = sb("s_id", [128, 128], BF16)
    ones2 = sb("ones2", [128, 2], BF16)
    onesr = sb("onesr", [1, 128], F32)
    mbias = sb("mbias", [128, 7], F32)
    mq2 = sb("mq2", [1, H], F32)
    mk2 = sb("mk2", [1, H], F32)
    redt = sb("redt", [1, 1], F32)
    bsc = sb("bsc", [1, H], F32)
    bsc560 = sb("bsc560", [1, H], F32)
    mtmp = sb("mtmp", [1, 512], F32)
    mneg = sb("mneg", [1, 512], BF16)
    ln_m = sb("ln_m", [128, 1], F32)
    ln_v = sb("ln_v", [128, 1], F32)
    ln_r = sb("ln_r", [128, 1], F32)
    wnat = [sb(f"wnat{j}", [128, D], BF16) for j in range(2)]

    base0 = nc.sbuf_base
    base = (base0 + 63) & ~63
    cursor = [base]
    placements = {}

    def place(region, name, shape, dt):
        t = nc.alloc_sbuf_tensor_at(name, shape, dt, offset=placements[region][1])
        sz = int(np.prod(shape[1:])) * mybir.dt.size(dt) if len(shape) > 1 else mybir.dt.size(dt)
        sz = shape[-1] * mybir.dt.size(dt)
        placements[region][1] += (sz + 63) & ~63
        assert placements[region][1] <= placements[region][0] + placements[region][2], \
            f"region {region} overflow at {name}"
        return t

    def region(name, size):
        placements[name] = [cursor[0], cursor[0], size]
        cursor[0] += size

    def reset_region(name):
        placements[name][1] = placements[name][0]

    KB = 1024
    region("R1", 32 * KB)
    region("R2", 34 * KB)
    region("R3", 93 * KB)
    region("R4", 38 * KB)
    assert cursor[0] <= nc.sbuf_top - 64, f"sbuf over: {cursor[0]} > {nc.sbuf_top}"

    qfm = [place("R1", f"qfm{j}", [128, LQ], BF16) for j in range(8)]
    reset_region("R1")
    xqfm = [place("R1", f"xqfm{j}", [128, LQ], BF16) for j in range(8)]
    attn = [place("R2", f"attn{j}", [128, LQ], BF16) for j in range(8)]
    reset_region("R2")
    hbuf = place("R2", "hbuf", [128, 16 * ICH], F32)
    hqtm = place("R2", "hqtm", [128, 16 * ICH], BF16)
    hqT = [place("R2", f"hqT{j}", [128, LQ], BF16) for j in range(2)]
    silu_t = place("R2", "silu_t", [128, ICH], F32)
    wT = [place("R3", f"wT{j}", [128, 3 * D], BF16) for j in range(8)]
    xs = [place("R3", f"xs{j}", [128, D], F32) for j in range(2)]
    xc = place("R3", "xc", [128, D], F32)
    sqt = place("R3", "sqt", [128, D], BF16)
    xlnt = place("R3", "xlnt", [128, D], BF16)
    xlnf = place("R3", "xlnf", [128, 8 * 512], BF16)
    cosw = [place("R3", f"cosw{j}", [128, 512], F32) for j in range(2)]
    sinw = [place("R3", f"sinw{j}", [128, 512], F32) for j in range(2)]
    rt1 = place("R3", "rt1", [128, 512], F32)
    rt2 = place("R3", "rt2", [128, 512], F32)
    kq_pre = place("R3", "kq_pre", [128, 512], BF16)
    sq2 = place("R3", "sq2", [128, 512], BF16)
    vt = place("R3", "vt", [128, 8 * VPW], BF16)
    A_KEYS = [f"wT{j}" for j in range(8)] + \
        ["xs0", "xs1", "xc", "sqt", "xlnt", "xlnf", "cosw0", "cosw1", "sinw0", "sinw1",
         "rt1", "rt2", "kq_pre", "sq2", "vt"]
    reset_region("R3")
    kax = [place("R3", f"kA{j}", [66, L], BF16) for j in range(2)]
    kbx = [place("R3", f"kB{j}", [66, L], BF16) for j in range(2)]
    qx = [place("R3", f"qx{j}", [66, LQ], BF16) for j in range(2)]
    vpair = [place("R3", f"vpair{j}", [128, 32 * VPW], BF16) for j in range(2)]
    pt = [place("R3", f"pt{j}", [128, 2048], BF16) for j in range(2)]
    rdrow = place("R3", "rdrow", [1, 512], F32)
    rdb = place("R3", "rdb", [64, 512], F32)
    wstr = [place("R3", "wstr0", [128, FI], F32)] * 2
    wq_stage = place("R3", "wq_stage", [128, FI], BF16)
    B_KEYS = ["kA0", "kA1", "kB0", "kB1", "qx0", "qx1", "vpair0", "vpair1",
              "pt0", "pt1", "rdrow", "rdb", "wstr0", "wq_stage"]
    reset_region("R3")
    y1 = [place("R3", f"y1_{t}", [128, D], F32) for t in range(16)]
    woutT = [place("R3", f"woutT{j}", [128, D], BF16) for j in range(8)]
    xln2 = place("R3", "xln2", [128, D], F32)
    resid = [place("R3", f"resid{j}", [128, D], F32) for j in range(2)]
    C_KEYS = [f"y1_{t}" for t in range(16)] + [f"woutT{j}" for j in range(8)] + \
        ["xln2", "resid0", "resid1"]
    placements["R3"][1] = placements["R3"][0] + 64 * KB
    wgqT = [[place("R3", f"wgqT{p}_{j}", [128, ICH], BF16) for j in range(8)] for p in range(2)]
    wuqT = [[place("R3", f"wuqT{p}_{j}", [128, ICH], BF16) for j in range(8)] for p in range(2)]
    wdqT = [[place("R3", f"wdqT{p}_{j}", [128, D], BF16) for j in range(2)] for p in range(2)]
    D_KEYS = [f"wgqT{p}_{j}" for p in range(2) for j in range(8)] + \
        [f"wuqT{p}_{j}" for p in range(2) for j in range(8)] + \
        [f"wdqT{p}_{j}" for p in range(2) for j in range(2)]
    q_ax = place("R4", "q_ax", [128, 2048], F32)
    q_a = place("R4", "q_a", [128, 2048], F32)
    q_sg = place("R4", "q_sg", [128, 2048], F32)
    q_t0 = place("R4", "q_t0", [128, 2048], BF16)
    q_t1 = place("R4", "q_t1", [128, 2048], BF16)
    q_bm = place("R4", "q_bm", [128, 128], F32)
    q_sc = place("R4", "q_sc", [128, 128], F32)
    q_rs = place("R4", "q_rs", [128, 128], F32)
    q_out = place("R4", "q_out", [128, 2048], BF16)

    P4 = nc.alloc_psum_tensor("P4", [128, 2048], F32)
    PB4 = nc.alloc_psum_tensor("PB4", [128, 512], F32)
    PB5 = nc.alloc_psum_tensor("PB5", [128, 512], F32)
    PB6 = nc.alloc_psum_tensor("PB6", [128, 512], F32)
    PB7 = nc.alloc_psum_tensor("PB7", [128, 512], F32)

    E("sync", "dma_start", s_id[:], ident, w=["id"])
    E("vector", "memset", ones2[:, :], 0.0, w=["ones2"])
    E("vector", "memset", ones2[0:64, 0:1], 1.0, r=["ones2"], w=["ones2"])
    E("vector", "memset", ones2[64:128, 1:2], 1.0, r=["ones2"], w=["ones2"])
    E("vector", "memset", onesr[:, :], 1.0, w=["onesr"])
    for i, m in enumerate([0.25, 0.75, 1.25, 1.75, 2.5, 3.5, 5.0]):
        E("vector", "memset", mbias[:, i:i + 1], -m, w=["mbias"])
    E("vector", "memset", mq2[:, :], 0.0, w=["mq2"])
    E("vector", "memset", mk2[:, :], 0.0, w=["mk2"])

    def transpose8(src_bf_ap, src_keys, store):
        for g in range(2):
            for k in range(4):
                dc = g * 4 + k
                E("tensor", "matmul", P4[:, k * 128:(k + 1) * 128],
                  src_bf_ap[:, dc * 128:(dc + 1) * 128], s_id[:],
                  start=True, stop=True, r=list(src_keys) + ["id"], w=["P4"])
            for k in range(4):
                store(g * 4 + k, P4[:, k * 128:(k + 1) * 128], "P4")

    def build_wT(tiles, dst_col_of):
        for t in tiles:
            wn = wnat[t % 2]
            wkey = f"wnat{t % 2}"
            E("gpsimd", "dma_start", wn[:], w_qkv[t * 128:(t + 1) * 128, :], w=[wkey])
            col = dst_col_of(t)
            def st(dc, ps, pk, _c=col):
                E("vector", "tensor_copy", wT[dc][:, _c:_c + 128], ps,
                  r=[pk], w=[f"wT{dc}"])
            transpose8(wn.ap(), [wkey], st)

    def build_rot():
        for dc in range(8):
            src = wT[dc].ap()
            s3 = src[:, 0:D].rearrange("p (h t) -> p h t", t=64)
            d3 = src[:, D:2 * D].rearrange("p (h t) -> p h t", t=64)
            E("vector", "tensor_scalar", d3[:, :, 0:32], s3[:, :, 32:64], -1.0, None,
              Alu.mult, r=[f"wT{dc}"], w=[f"wT{dc}"])
            E("vector", "tensor_copy", d3[:, :, 32:64], s3[:, :, 0:32],
              r=[f"wT{dc}"], w=[f"wT{dc}"])

    def ln_tile(src_ap, src_keys, dst_ap, dst_key, xc_ap=None, xc_key="xc",
                sq_ap=None, sq_key="sqt"):
        if xc_ap is None:
            xc_ap = xc[:]
        if sq_ap is None:
            sq_ap = sqt[:]
        E("vector", "tensor_reduce", ln_m[:], src_ap, AX.X, Alu.add, r=src_keys, w=["ln_m"])
        E("vector", "tensor_scalar", ln_m[:], ln_m[:], 1.0 / D, None, Alu.mult,
          r=["ln_m"], w=["ln_m"])
        E("vector", "tensor_scalar", xc_ap, src_ap, ln_m[:], None, Alu.subtract,
          r=src_keys + ["ln_m"], w=[xc_key])
        E("scalar", "activation", sq_ap, xc_ap, Act.Square, accum_out=ln_v[:],
          r=[xc_key], w=[sq_key, "ln_v"])
        E("vector", "tensor_scalar", ln_v[:], ln_v[:], 1.0 / D, 1e-5, Alu.mult, Alu.add,
          r=["ln_v"], w=["ln_v"])
        E("scalar", "activation", ln_v[:], ln_v[:], Act.Sqrt, r=["ln_v"], w=["ln_v"])
        E("vector", "reciprocal", ln_r[:], ln_v[:], r=["ln_v"], w=["ln_r"])
        E("vector", "tensor_scalar", dst_ap, xc_ap, ln_r[:], None, Alu.mult,
          r=[xc_key, "ln_r"], w=[dst_key])

    def build_xlnf(x_src, tok0_base):
        for t in range(4):
            tok0 = tok0_base + t * 128
            xsb = xs[t % 2]
            xkey = f"xs{t % 2}"
            E("gpsimd", "dma_start", xsb[:], x_src[tok0:tok0 + 128, :], w=[xkey])
            ln_tile(xsb[:], [xkey], xlnt[:], "xlnt")
            if DEBUG and tok0_base == 0 and t == 0 and x_src is x_all:
                E("sync", "dma_start", dbg_xs, xsb[:], r=[xkey])
                E("sync", "dma_start", dbg_xc, xc[:], r=["xc"])
                E("sync", "dma_start", dbg_xlnt, xlnt[:], r=["xlnt"])
                E("sync", "dma_start", dbg_id, s_id[:], r=["id"])
                E("sync", "dma_start", dbg_lnv[:, 0:1], ln_v[:], r=["ln_v"],
                  allow_slow_non_contiguous=True)
                E("sync", "dma_start", dbg_lnv[:, 1:2], ln_r[:], r=["ln_r"],
                  allow_slow_non_contiguous=True)
            def stx(dc, ps, pk, _t=t):
                E("vector", "tensor_copy",
                  xlnf[:, dc * 512 + _t * 128: dc * 512 + (_t + 1) * 128], ps,
                  r=[pk], w=["xlnf"])
            transpose8(xlnt.ap(), ["xlnt"], stx)

    def gen_rope(fc, cw_ap, sw_ap, cw_keys, out_ap, out_key, m2t, m2key, dbg=None):
        for dc in range(8):
            E("tensor", "matmul", PB4[:, :], wT[dc][:, fc * 128:(fc + 1) * 128],
              xlnf[:, dc * 512:(dc + 1) * 512], start=(dc == 0), stop=(dc == 7),
              r=[f"wT{dc}", "xlnf"], w=["PB4"])
        for dc in range(8):
            E("tensor", "matmul", PB5[:, :], wT[dc][:, D + fc * 128: D + (fc + 1) * 128],
              xlnf[:, dc * 512:(dc + 1) * 512], start=(dc == 0), stop=(dc == 7),
              r=[f"wT{dc}", "xlnf"], w=["PB5"])
        if dbg is not None:
            E("vector", "tensor_copy", rt1[:], PB4[:, :], r=["PB4"], w=["rt1"])
            E("sync", "dma_start", dbg[0], rt1[:], r=["rt1"])
            E("vector", "tensor_copy", rt1[:], PB5[:, :], r=["PB5"], w=["rt1"])
            E("sync", "dma_start", dbg[1], rt1[:], r=["rt1"])
        E("vector", "tensor_tensor", rt1[:], PB4[:, :], cw_ap, Alu.mult,
          r=["PB4"] + cw_keys, w=["rt1"])
        E("vector", "tensor_tensor", rt2[:], PB5[:, :], sw_ap, Alu.mult,
          r=["PB5"] + cw_keys, w=["rt2"])
        E("vector", "tensor_tensor", out_ap, rt1[:], rt2[:], Alu.add,
          r=["rt1", "rt2"], w=[out_key])
        E("vector", "tensor_tensor", sq2[:], out_ap, out_ap, Alu.mult,
          r=[out_key], w=["sq2"])
        for par2 in range(2):
            pnb = PB6 if par2 == 0 else PB7
            pnk = "PB6" if par2 == 0 else "PB7"
            E("tensor", "matmul", pnb[0:1, :], ones2[:, par2:par2 + 1], sq2[:],
              start=True, stop=True, r=["ones2", "sq2"], w=[pnk])
        for par2 in range(2):
            hd = 2 * fc + par2
            pnb = PB6 if par2 == 0 else PB7
            pnk = "PB6" if par2 == 0 else "PB7"
            E("vector", "tensor_reduce", redt[:], pnb[0:1, :], AX.X, Alu.max,
              r=[pnk], w=["redt"])
            E("vector", "tensor_tensor", m2t[0:1, hd:hd + 1], m2t[0:1, hd:hd + 1],
              redt[:], Alu.max, r=["redt", m2key], w=[m2key])

    build_wT(range(8, 16), lambda t: (t - 8) * 128)
    build_wT(range(16, 24), lambda t: 2 * D + (t - 16) * 128)
    build_rot()
    for c in range(8):
        cw0 = c * 512
        par = c % 2
        cwb, swb = cosw[par], sinw[par]
        E("sync", "dma_start", cwb[:], cos_f[:, cw0:cw0 + 512], w=[f"cosw{par}"])
        E("sync", "dma_start", swb[:], sin_f[:, cw0:cw0 + 512], w=[f"sinw{par}"])
        build_xlnf(x_all, cw0)
        if DEBUG and c == 0:
            E("sync", "dma_start", dbg_xlnf, xlnf[:], r=["xlnf"])
        for fc in range(8):
            gen_rope(fc, cwb[:], swb[:], [f"cosw{par}", f"sinw{par}"],
                     kq_pre[:], "kq_pre", mk2, "mk2",
                     dbg=(dbg_kpre if (DEBUG and c == 0 and fc == 0) else None))
            E("sync", "dma_start", kscr[fc, :, cw0:cw0 + 512], kq_pre[:],
              r=["kq_pre"], w=[f"kscr{fc}"])
        for t in range(4):
            for half in range(2):
                pb = PB7 if half == 0 else PB4
                pk = "PB7" if half == 0 else "PB4"
                for dc in range(8):
                    E("tensor", "matmul", pb[:, :],
                      xlnf[:, dc * 512 + t * 128: dc * 512 + (t + 1) * 128],
                      wT[dc][:, 2 * D + half * 512: 2 * D + (half + 1) * 512],
                      start=(dc == 0), stop=(dc == 7),
                      r=[f"wT{dc}", "xlnf"], w=[pk])
            v4 = vt.ap().rearrange("p (h w) -> p h w", w=66)
            E("vector", "memset", v4[:, :, 64:66], 0.0, w=["vt"])
            E("vector", "memset", v4[:, :, 64:65], 1.0, r=["vt"], w=["vt"])
            for half in range(2):
                pb = PB7 if half == 0 else PB4
                pk = "PB7" if half == 0 else "PB4"
                srcv = pb.ap().rearrange("p (hh w) -> p hh w", w=64)
                base_c = 4 * half * VPW
                dstv = vt.ap()[:, base_c: base_c + 4 * VPW].rearrange(
                    "p (hh w) -> p hh w", w=66)[:, :, 0:64]
                E("vector", "tensor_copy", dstv, srcv, r=[pk], w=["vt"])
            for pr in range(8):
                E("gpsimd", "dma_start", vscr[pr, c * 4 + t, :, :],
                  vt[:, pr * VPW:(pr + 1) * VPW], r=["vt"], w=[f"vscr{pr}"])
    build_wT(range(0, 8), lambda t: t * 128)
    build_rot()
    for c in range(4):
        cw0 = c * 512
        par = c % 2
        cwb, swb = cosw[par], sinw[par]
        E("sync", "dma_start", cwb[:], cos_o[:, cw0:cw0 + 512], w=[f"cosw{par}"])
        E("sync", "dma_start", swb[:], sin_o[:, cw0:cw0 + 512], w=[f"sinw{par}"])
        build_xlnf(x_own, cw0)
        for fc in range(8):
            gen_rope(fc, cwb[:], swb[:], [f"cosw{par}", f"sinw{par}"],
                     qfm[fc][:, cw0:cw0 + 512], f"qfm{fc}", mq2, "mq2")
            E("sync", "dma_start", qscr[2 * fc, :, cw0:cw0 + 512],
              qfm[fc][0:64, cw0:cw0 + 512], r=[f"qfm{fc}"], w=["qscr"])
            E("sync", "dma_start", qscr[2 * fc + 1, :, cw0:cw0 + 512],
              qfm[fc][64:128, cw0:cw0 + 512], r=[f"qfm{fc}"], w=["qscr"])

    E("vector", "tensor_tensor", bsc[0:1, :], mq2[0:1, :], mk2[0:1, :], Alu.mult,
      r=["mq2", "mk2"], w=["bsc"])
    E("scalar", "activation", bsc[0:1, :], bsc[0:1, :], Act.Sqrt, r=["bsc"], w=["bsc"])
    E("vector", "tensor_scalar", bsc560[0:1, :], bsc[0:1, :], 1.02, -560.0,
      Alu.mult, Alu.add, r=["bsc"], w=["bsc560"])

    build_wT(range(8, 16), lambda t: (t - 8) * 128)
    build_rot()
    for c in range(4):
        cw0 = c * 512
        par = c % 2
        cwb, swb = cosw[par], sinw[par]
        E("sync", "dma_start", cwb[:], cos_o[:, cw0:cw0 + 512], w=[f"cosw{par}"])
        E("sync", "dma_start", swb[:], sin_o[:, cw0:cw0 + 512], w=[f"sinw{par}"])
        build_xlnf(x_own, cw0)
        for fc in range(8):
            for dc in range(8):
                E("tensor", "matmul", PB4[:, :], wT[dc][:, fc * 128:(fc + 1) * 128],
                  xlnf[:, dc * 512:(dc + 1) * 512], start=(dc == 0), stop=(dc == 7),
                  r=[f"wT{dc}", "xlnf"], w=["PB4"])
            for dc in range(8):
                E("tensor", "matmul", PB5[:, :], wT[dc][:, D + fc * 128: D + (fc + 1) * 128],
                  xlnf[:, dc * 512:(dc + 1) * 512], start=(dc == 0), stop=(dc == 7),
                  r=[f"wT{dc}", "xlnf"], w=["PB5"])
            E("vector", "tensor_tensor", rt1[:], PB4[:, :], cwb[:], Alu.mult,
              r=["PB4", f"cosw{par}"], w=["rt1"])
            E("vector", "tensor_tensor", rt2[:], PB5[:, :], swb[:], Alu.mult,
              r=["PB5", f"sinw{par}"], w=["rt2"])
            E("vector", "tensor_tensor", kq_pre[:], rt1[:], rt2[:], Alu.add,
              r=["rt1", "rt2"], w=["kq_pre"])
            E("vector", "tensor_tensor", sq2[:], qfm[fc][:, cw0:cw0 + 512], kq_pre[:],
              Alu.mult, r=[f"qfm{fc}", "kq_pre"], w=["sq2"])
            for par2 in range(2):
                h = 2 * fc + par2
                E("tensor", "matmul", PB6[0:1, :], ones2[:, par2:par2 + 1], sq2[:],
                  start=True, stop=True, r=["ones2", "sq2"], w=["PB6"])
                E("vector", "tensor_scalar", mtmp[:], PB6[0:1, :],
                  bsc560[0:1, h:h + 1], None, Alu.max, r=["PB6", "bsc560"], w=["mtmp"])
                E("vector", "tensor_scalar", mneg[:], mtmp[:], -1.0, None, Alu.mult,
                  r=["mtmp"], w=["mneg"])
                E("sync", "dma_start", mscr[h, cw0:cw0 + 512], mneg[:],
                  r=["mneg"], w=["mscr"])

    MIDS = [0.25, 0.75, 1.25, 1.75, 2.5, 3.5, 5.0]
    STEP = [0.5, 0.5, 0.5, 0.5, 1.0, 1.0, 2.0]

    def fp4_quant(src_ap, src_keys, n, out_ap, out_key, ladder="vector"):
        assert n <= 2048 and n % 16 == 0
        nb = n // 16
        E("vector", "tensor_scalar", q_sg[:, 0:n], src_ap, 0.0, 2.0, Alu.is_ge, Alu.mult,
          r=src_keys, w=["q_sg"])
        E("vector", "scalar_tensor_tensor", q_ax[:, 0:n], q_sg[:, 0:n], 1.0, src_ap,
          Alu.subtract, Alu.mult, r=["q_sg"] + src_keys, w=["q_ax"])
        ax3 = q_ax.ap()[:, 0:n].rearrange("p (b k) -> p b k", k=16)
        E("vector", "tensor_reduce", q_bm[:, 0:nb], ax3, AX.X, Alu.max,
          r=["q_ax"], w=["q_bm"])
        E("vector", "tensor_scalar", q_sc[:, 0:nb], q_bm[:, 0:nb], 1.0 / 6.0, 1e-12,
          Alu.mult, Alu.max, r=["q_bm"], w=["q_sc"])
        E("vector", "reciprocal", q_rs[:, 0:nb], q_sc[:, 0:nb], r=["q_sc"], w=["q_rs"])
        rs3 = q_rs.ap()[:, 0:nb].unsqueeze(2).broadcast_to([128, nb, 16])
        a3 = q_a.ap()[:, 0:n].rearrange("p (b k) -> p b k", k=16)
        E("vector", "tensor_tensor", a3, ax3, rs3, Alu.mult, r=["q_ax", "q_rs"], w=["q_a"])
        if ladder == "vector":
            E("vector", "tensor_scalar", q_t0[:, 0:n], q_a[:, 0:n], MIDS[0], STEP[0],
              Alu.is_gt, Alu.mult, r=["q_a"], w=["q_t0"])
            for i in range(1, 7):
                E("vector", "tensor_scalar", q_t1[:, 0:n], q_a[:, 0:n], MIDS[i], STEP[i],
                  Alu.is_gt, Alu.mult, r=["q_a"], w=["q_t1"])
                E("vector", "tensor_tensor", q_t0[:, 0:n], q_t0[:, 0:n], q_t1[:, 0:n],
                  Alu.add, r=["q_t1", "q_t0"], w=["q_t0"])
            E("vector", "tensor_scalar", q_out[:, 0:n], q_sg[:, 0:n], 1.0, None,
              Alu.subtract, r=["q_sg"], w=["q_outs"])
            E("vector", "tensor_tensor", q_t1[:, 0:n], q_t0[:, 0:n], q_out[:, 0:n],
              Alu.mult, r=["q_t0", "q_outs"], w=["q_t1"])
        else:
            for i in range(7):
                E("scalar", "activation", q_out[:, 0:n], q_a[:, 0:n], Act.Sign,
                  bias=mbias[:, i:i + 1], r=["q_a", "mbias"], w=["q_outs"])
                if i == 0:
                    E("vector", "tensor_scalar", q_t0[:, 0:n], q_out[:, 0:n],
                      STEP[0] / 2.0, None, Alu.mult, r=["q_outs"], w=["q_t0"])
                else:
                    E("vector", "scalar_tensor_tensor", q_t0[:, 0:n], q_out[:, 0:n],
                      STEP[i] / 2.0, q_t0[:, 0:n], Alu.mult, Alu.add,
                      r=["q_outs", "q_t0"], w=["q_t0"])
            E("vector", "tensor_scalar", q_t0[:, 0:n], q_t0[:, 0:n], 3.0, None,
              Alu.add, r=["q_t0"], w=["q_t0"])
            E("vector", "tensor_scalar", q_out[:, 0:n], q_sg[:, 0:n], 1.0, None,
              Alu.subtract, r=["q_sg"], w=["q_outs"])
            E("vector", "tensor_tensor", q_t1[:, 0:n], q_t0[:, 0:n], q_out[:, 0:n],
              Alu.mult, r=["q_t0", "q_outs"], w=["q_t1"])
        sc3 = q_sc.ap()[:, 0:nb].unsqueeze(2).broadcast_to([128, nb, 16])
        st3 = q_t1.ap()[:, 0:n].rearrange("p (b k) -> p b k", k=16)
        o3 = out_ap.rearrange("p (b k) -> p b k", k=16)
        E("vector", "tensor_tensor", o3, st3, sc3, Alu.mult,
          r=["q_t1", "q_sc"], w=[out_key])

    wtiles = [("g", i) for i in range(FI // 128)] + \
             [("u", i) for i in range(FI // 128)] + \
             [("d", i) for i in range(D // 128)]

    def emit_wtile(kind, ti):
        wn = wstr[0]
        wkey = "wstr0"
        if kind == "g":
            src, n, scr, skey = w_gate[ti * 128:(ti + 1) * 128, :], D, wgscr, "wgscr"
        elif kind == "u":
            src, n, scr, skey = w_up[ti * 128:(ti + 1) * 128, :], D, wuscr, "wuscr"
        else:
            src, n, scr, skey = w_down[ti * 128:(ti + 1) * 128, :], FI, wdscr, "wdscr"
        E("sync", "dma_start", wn[:, 0:n], src, w=[wkey])
        pos = 0
        while pos < n:
            seg = min(2048, n - pos)
            fp4_quant(wn.ap()[:, pos:pos + seg], [wkey], seg,
                      wq_stage.ap()[:, pos:pos + seg], "wq_stage", ladder="vector")
            pos += seg
        nblk = n // 128
        for g in range((nblk + 3) // 4):
            lo, hi = g * 4, min(g * 4 + 4, nblk)
            for k in range(lo, hi):
                E("tensor", "matmul", PB4[:, (k - lo) * 128:(k - lo + 1) * 128],
                  wq_stage[:, k * 128:(k + 1) * 128], s_id[:], start=True, stop=True,
                  r=["wq_stage", "id"], w=["PB4"])
            for k in range(lo, hi):
                dst = scr[k, :, ti * 128:(ti + 1) * 128] if kind == "d" else \
                    scr[k, :, ti * 128:(ti + 1) * 128]
                E("vector", "tensor_copy", q_out[:, (k % 16) * 128:((k % 16) + 1) * 128],
                  PB4[:, (k - lo) * 128:(k - lo + 1) * 128], r=["PB4"], w=["q_out"])
                E("sync", "dma_start", dst,
                  q_out[:, (k % 16) * 128:((k % 16) + 1) * 128], r=["q_out"], w=[skey])

    if DEBUG:
        for j in range(8):
            E("sync", "dma_start", dbg_q[j], qfm[j][:], r=[f"qfm{j}"])
        E("sync", "dma_start", dbg_bias[0:1, :], bsc[0:1, :], r=["bsc"])

    P.alias_seed(B_KEYS, A_KEYS)

    for j in range(2):
        E("vector", "memset", kax[j][64:65, :], 1.0, w=[f"kA{j}"])
        E("vector", "memset", kbx[j][64:65, :], 1.0, w=[f"kB{j}"])

    def load_pair(pr):
        vb = vpair[pr % 2]
        E("sync", "dma_start", kax[pr % 2][0:64, :], kscr[pr, 0:64, :],
          r=[f"kscr{j}" for j in range(8)], w=[f"kA{pr % 2}"])
        E("sync", "dma_start", kbx[pr % 2][0:64, :], kscr[pr, 64:128, :],
          r=[f"kscr{j}" for j in range(8)], w=[f"kB{pr % 2}"])
        for seg in range(4):
            srcv = vscr[pr, seg * 8:(seg + 1) * 8, :, :].transpose([1, 0, 2])
            dstv = vb.ap()[:, seg * 8 * VPW:(seg + 1) * 8 * VPW].rearrange(
                "p (c w) -> p c w", w=VPW)
            E("gpsimd", "dma_start", dstv, srcv,
              r=[f"vscr{pr}"], w=[f"vpair{pr % 2}"])

    def load_q(h):
        qb = qx[h % 2]
        E("sync", "dma_start", qb[0:64, :], qscr[h, :, :], r=["qscr"], w=[f"qx{h % 2}"])
        E("sync", "dma_start", qb[64:65, :], mscr[h:h + 1, :], r=["mscr"], w=[f"qx{h % 2}"])

    NW = len(wtiles)
    wpos = 0
    load_pair(0)
    load_q(0)
    for pr in range(8):
        if pr < 7:
            load_pair(pr + 1)
        vkey = f"vpair{pr % 2}"
        vb = vpair[pr % 2]
        if DEBUG and pr == 0:
            E("sync", "dma_start", dbg_kp[0:64, :], kax[0][0:64, :], r=["kA0"])
            E("sync", "dma_start", dbg_kp[64:128, :], kbx[0][0:64, :], r=["kB0"])
            E("sync", "dma_start", dbg_vp, vb[:], r=[vkey])
        for hh in range(2):
            h = 2 * pr + hh
            if h < H - 1:
                load_q(h + 1)
            off = hh * 64
            kx = kax[pr % 2] if hh == 0 else kbx[pr % 2]
            kkey = (f"kA{pr % 2}" if hh == 0 else f"kB{pr % 2}")
            qb = qx[h % 2]
            qkey = f"qx{h % 2}"
            for qc in range(4):
                q0 = qc * 512
                for g4 in range(8):
                    ptb = pt[g4 % 2]
                    ptkey = f"pt{g4 % 2}"
                    for j in range(4):
                        kc = g4 * 4 + j
                        E("tensor", "matmul", P4[:, j * 512:(j + 1) * 512],
                          kx[0:65, kc * 128:(kc + 1) * 128],
                          qb[0:65, q0:q0 + 512],
                          start=True, stop=True, r=[kkey, qkey], w=["P4"])
                    if DEBUG and pr == 0 and hh == 0 and qc == 0 and g4 == 0:
                        E("vector", "tensor_copy", q_sg[:, 0:512], P4[:, 0:512], r=["P4"], w=["q_sg"])
                        E("sync", "dma_start", dbg_st[:, 0:512], q_sg[:, 0:512], r=["q_sg"])
                    E("scalar", "activation", ptb[:], P4[:, :], Act.Exp, scale=0.125,
                      r=["P4"], w=[ptkey])
                    if DEBUG and pr == 0 and hh == 0 and qc == 0 and g4 == 0:
                        E("sync", "dma_start", dbg_pt, ptb[:], r=[ptkey])
                    for j in range(4):
                        kc = g4 * 4 + j
                        vsl = vb.ap()[:, kc * VPW + hh * 66: kc * VPW + hh * 66 + 65]
                        E("tensor", "matmul", PB4[0:65, :], vsl,
                          ptb[:, j * 512:(j + 1) * 512],
                          start=(kc == 0), stop=(kc == 31),
                          r=[vkey, ptkey], w=["PB4"])
                if DEBUG and pr == 0 and hh == 0 and qc == 0:
                    E("vector", "tensor_copy", rdb[:], PB4[0:64, :], r=["PB4"], w=["rdb"])
                    E("sync", "dma_start", dbg_av, rdb[:], r=["rdb"])
                    E("vector", "tensor_copy", rdb[0:1, :], PB4[64:65, :], r=["PB4"], w=["rdb"])
                    E("sync", "dma_start", dbg_avd, rdb[0:1, :], r=["rdb"])
                E("vector", "reciprocal", rdrow[:], PB4[64:65, :], r=["PB4"], w=["rdrow"])
                E("tensor", "matmul", PB5[0:64, :], onesr[0:1, 0:64], rdrow[:],
                  start=True, stop=True, r=["onesr", "rdrow"], w=["PB5"])
                E("vector", "tensor_copy", rdb[:], PB5[0:64, :], r=["PB5"], w=["rdb"])
                E("vector", "tensor_tensor", attn[pr][off:off + 64, q0:q0 + 512],
                  PB4[0:64, :], rdb[:], Alu.mult, r=["PB4", "rdb"], w=[f"attn{pr}"])
        wend = (pr + 1) * NW // 8
        while wpos < wend:
            emit_wtile(*wtiles[wpos])
            wpos += 1

    if DEBUG:
        for j in range(8):
            E("sync", "dma_start", dbg_attn[j], attn[j][:], r=[f"attn{j}"])

    P.alias_seed(C_KEYS, B_KEYS)
    P.alias_seed([f"xqfm{j}" for j in range(8)], [f"qfm{j}" for j in range(8)])
    for t in range(8):
        wn = wnat[t % 2]
        wkey = f"wnat{t % 2}"
        E("gpsimd", "dma_start", wn[:], w_out[t * 128:(t + 1) * 128, :], w=[wkey])
        def stw(dc, ps, pk, _t=t):
            E("vector", "tensor_copy", woutT[dc][:, _t * 128:(_t + 1) * 128], ps,
              r=[pk], w=[f"woutT{dc}"])
        transpose8(wn.ap(), [wkey], stw)

    for tt in range(16):
        rb = resid[tt % 2]
        rkey = f"resid{tt % 2}"
        E("gpsimd", "dma_start", rb[:], x_own[tt * 128:(tt + 1) * 128, :], w=[rkey])
        for half in range(2):
            pb = PB6 if half == 0 else PB7
            pk = "PB6" if half == 0 else "PB7"
            for dc in range(8):
                E("tensor", "matmul", pb[:, :], attn[dc][:, tt * 128:(tt + 1) * 128],
                  woutT[dc][:, half * 512:(half + 1) * 512],
                  start=(dc == 0), stop=(dc == 7),
                  r=[f"attn{dc}", f"woutT{dc}"], w=[pk])
            E("vector", "tensor_tensor", y1[tt][:, half * 512:(half + 1) * 512],
              pb[:, :], rb[:, half * 512:(half + 1) * 512], Alu.add,
              r=[pk, rkey], w=[f"y1_{tt}"])
        ln_tile(y1[tt][:], [f"y1_{tt}"], xln2[:], "xln2",
                xc_ap=q_ax[:, 1024:2048], xc_key="q_ax",
                sq_ap=q_t1[:, 1024:2048], sq_key="q_t1")
        fp4_quant(xln2.ap()[:], ["xln2"], D, q_out.ap()[:, 0:D], "q_out", ladder="vector")
        def stq(dc, ps, pk, _tt=tt):
            E("vector", "tensor_copy", xqfm[dc][:, _tt * 128:(_tt + 1) * 128], ps,
              r=[pk], w=[f"xqfm{dc}"])
        transpose8(q_out.ap()[:, 0:D], ["q_out"], stq)

    if DEBUG:
        for t in range(16):
            E("sync", "dma_start", dbg_y1[t], y1[t][:], r=[f"y1_{t}"])
        for j in range(8):
            E("sync", "dma_start", dbg_xq[j], xqfm[j][:], r=[f"xqfm{j}"])

    P.alias_seed(D_KEYS, ["woutT0", "woutT7", "xln2", "resid0", "resid1"])
    P.alias_seed(["hbuf", "hqtm", "hqT0", "hqT1", "silu_t"],
                 [f"attn{j}" for j in range(8)])
    for ic in range(NIC):
        par = ic % 2
        for dc in range(8):
            E("sync", "dma_start", wgqT[par][dc][:], wgscr[dc, :, ic * ICH:(ic + 1) * ICH],
              r=["wgscr"], w=[f"wgqT{par}_{dc}"])
            E("sync", "dma_start", wuqT[par][dc][:], wuscr[dc, :, ic * ICH:(ic + 1) * ICH],
              r=["wuscr"], w=[f"wuqT{par}_{dc}"])
        for ip in range(2):
            E("sync", "dma_start", wdqT[par][ip][:], wdscr[ic * 2 + ip, :, :],
              r=["wdscr"], w=[f"wdqT{par}_{ip}"])
        for tt in range(16):
            for dc in range(8):
                E("tensor", "matmul", PB6[:, 0:ICH], xqfm[dc][:, tt * 128:(tt + 1) * 128],
                  wgqT[par][dc][:], start=(dc == 0), stop=(dc == 7),
                  r=[f"xqfm{dc}", f"wgqT{par}_{dc}"], w=["PB6"])
            for dc in range(8):
                E("tensor", "matmul", PB7[:, 0:ICH], xqfm[dc][:, tt * 128:(tt + 1) * 128],
                  wuqT[par][dc][:], start=(dc == 0), stop=(dc == 7),
                  r=[f"xqfm{dc}", f"wuqT{par}_{dc}"], w=["PB7"])
            if DEBUG and ic == 0 and tt == 0:
                E("vector", "tensor_copy", q_sg[:, 0:ICH], PB6[:, 0:ICH], r=["PB6"], w=["q_sg"])
                E("sync", "dma_start", dbg_gt, q_sg[:, 0:ICH], r=["q_sg"])
            E("scalar", "activation", silu_t[:], PB6[:, 0:ICH], Act.Silu,
              r=["PB6"], w=["silu_t"])
            E("vector", "tensor_tensor", hbuf[:, tt * ICH:(tt + 1) * ICH], silu_t[:],
              PB7[:, 0:ICH], Alu.mult, r=["silu_t", "PB7"], w=["hbuf"])
        for hf in range(2):
            fp4_quant(hbuf.ap()[:, hf * 2048:(hf + 1) * 2048], ["hbuf"], 2048,
                      hqtm.ap()[:, hf * 2048:(hf + 1) * 2048], "hqtm", ladder="vector")
        if DEBUG and ic == 0:
            E("sync", "dma_start", dbg_hb, hbuf[:], r=["hbuf"])
            E("sync", "dma_start", dbg_hq, hqtm[:], r=["hqtm"])
            E("sync", "dma_start", dbg_wg, wgqT[0][0][:], r=["wgqT0_0"])
        for tt in range(16):
            for ip in range(2):
                k = (tt * 2 + ip) % 4
                E("tensor", "matmul", P4[:, k * 128 + 1024:(k + 1) * 128 + 1024],
                  hqtm[:, tt * ICH + ip * 128: tt * ICH + (ip + 1) * 128], s_id[:],
                  start=True, stop=True, r=["hqtm", "id"], w=["P4"])
                E("vector", "tensor_copy", hqT[ip][:, tt * 128:(tt + 1) * 128],
                  P4[:, k * 128 + 1024:(k + 1) * 128 + 1024], r=["P4"], w=[f"hqT{ip}"])
        for tt in range(16):
            for half in range(2):
                pb = PB6 if half == 0 else PB7
                pk = "PB6" if half == 0 else "PB7"
                for ip in range(2):
                    E("tensor", "matmul", pb[:, :], hqT[ip][:, tt * 128:(tt + 1) * 128],
                      wdqT[par][ip][:, half * 512:(half + 1) * 512],
                      start=(ip == 0), stop=(ip == 1),
                      r=[f"hqT{ip}", f"wdqT{par}_{ip}"], w=[pk])
                E("vector", "tensor_tensor", y1[tt][:, half * 512:(half + 1) * 512],
                  y1[tt][:, half * 512:(half + 1) * 512], pb[:, :], Alu.add,
                  r=[pk, f"y1_{tt}"], w=[f"y1_{tt}"])
        if DEBUG and ic < 3:
            E("sync", "dma_start", dbg_y15[ic], y1[15][:], r=["y1_15"])
            if ic == 0:
                E("sync", "dma_start", dbg_y15[3], y1[14][:], r=["y1_14"])
                E("sync", "dma_start", dbg_hqt, hqT[0][:], r=["hqT0"])

    for tt in range(16):
        E("gpsimd", "dma_start", out[tt * 128:(tt + 1) * 128, :], y1[tt][:],
          r=[f"y1_{tt}"])

    P.finalize()
    return nc


_NC = None

def _get_nc():
    global _NC
    if _NC is None:
        _NC = build_program()
    return _NC


def _tables():
    pos = np.arange(L, dtype=np.float64)
    inv = 1.0 / (10000.0 ** (np.arange(32, dtype=np.float64) / 32.0))
    invf = np.tile(inv, 4)
    ang = invf[:, None] * pos[None, :]
    return np.cos(ang).astype(np.float32), np.sin(ang).astype(np.float32)


def kernel(L0_attn_x, L0_attn_w_qkv, L0_attn_w_out, L0_ffn_w_gate, L0_ffn_w_up, L0_ffn_w_down):
    X = np.ascontiguousarray(np.asarray(L0_attn_x, dtype=np.float32))
    wqkv = np.ascontiguousarray(np.asarray(L0_attn_w_qkv, dtype=np.float32))
    wout = np.ascontiguousarray(np.asarray(L0_attn_w_out, dtype=np.float32))
    wg = np.ascontiguousarray(np.asarray(L0_ffn_w_gate, dtype=np.float32))
    wu = np.ascontiguousarray(np.asarray(L0_ffn_w_up, dtype=np.float32))
    wd = np.ascontiguousarray(np.asarray(L0_ffn_w_down, dtype=np.float32))
    cos_f, sin_f = _tables()
    ident = np.eye(128, dtype=np.float32).astype(ml_dtypes.bfloat16)

    nc = _get_nc()
    in_maps = []
    for c in range(8):
        b, hf = c // 2, c % 2
        sl = slice(hf * LQ, (hf + 1) * LQ)
        in_maps.append({
            "x_all": X[b],
            "x_own": np.ascontiguousarray(X[b, sl]),
            "w_qkv": wqkv, "w_out": wout,
            "w_gate": wg, "w_up": wu, "w_down": wd,
            "cos_f": cos_f, "sin_f": sin_f,
            "cos_o": np.ascontiguousarray(cos_f[:, sl]),
            "sin_o": np.ascontiguousarray(sin_f[:, sl]),
            "ident": ident,
        })
    res = run_bass_kernel_spmd(nc, in_maps, core_ids=list(range(8)))
    outv = np.empty((4, L, D), dtype=np.float32)
    for c in range(8):
        b, hf = c // 2, c % 2
        outv[b, hf * LQ:(hf + 1) * LQ] = res.results[c]["out"]
    return outv


# version 05
# speedup vs baseline: 1.2560x; 1.2560x over previous
import numpy as np
import ml_dtypes
import concourse.bass as bass
import concourse.mybir as mybir
from concourse.bass_utils import run_bass_kernel_spmd

F32 = mybir.dt.float32
BF16 = mybir.dt.bfloat16
Alu = mybir.AluOpType
Act = mybir.ActivationFunctionType
AX = mybir.AxisListType

D = 1024
L = 4096
LQ = 2048
H = 16
FI = 2816
ICH = 256
NIC = FI // ICH
VPW = 132


class Buf:
    __slots__ = ("ready", "users")
    def __init__(self):
        self.ready = []
        self.users = []


class Prog:
    ENGINES = ("sync", "gpsimd", "vector", "scalar", "tensor")
    DMA_ENG = {"sync", "gpsimd"}

    def __init__(self, nc):
        self.nc = nc
        self.ops = []
        self.count = {e: 0 for e in self.ENGINES}
        self.B = {}

    def buf(self, key):
        if key not in self.B:
            self.B[key] = Buf()
        return self.B[key]

    def alias_seed(self, new_keys, old_keys):
        toks = []
        for ok in old_keys:
            b = self.buf(ok)
            toks += b.ready + b.users
        for nk in new_keys:
            self.buf(nk).users.extend(toks)

    def emit(self, eng, meth, *args, r=(), w=(), **kw):
        inc = 16 if eng in self.DMA_ENG else 1
        if SERIAL:
            waits = {e: c for e, c in self.count.items() if c > 0 and e != eng}
            self.count[eng] += inc
            tok = (eng, self.count[eng])
            for b in [self.buf(k) for k in w]:
                b.ready = [tok]; b.users = []
            for b in [self.buf(k) for k in r]:
                b.users.append(tok)
            self.ops.append((eng, meth, args, kw, tuple(waits.items()), inc))
            return tok
        waits = {}
        def add(tok, raw=False):
            e, v = tok
            if e != eng:
                waits[e] = max(waits.get(e, 0), v)
            elif raw and eng == "vector" and (self.count[eng] + inc - v) <= 1:
                waits[e] = max(waits.get(e, 0), v)
            elif raw and eng in self.DMA_ENG:
                waits[e] = max(waits.get(e, 0), v)
        reads = [self.buf(k) for k in r]
        writes = [self.buf(k) for k in w]
        for b in reads:
            for t in b.ready:
                add(t, raw=True)
        for b in writes:
            for t in b.ready:
                add(t)
            for t in b.users:
                add(t)
        self.count[eng] += inc
        tok = (eng, self.count[eng])
        for b in writes:
            b.ready = [tok]
            b.users = []
        for b in reads:
            b.users.append(tok)
        self.ops.append((eng, meth, args, kw, tuple(waits.items()), inc))
        return tok

    def finalize(self):
        nc = self.nc
        final = [(e, self.count[e]) for e in self.ENGINES if self.count[e] > 0]
        from contextlib import ExitStack
        stack = ExitStack()
        sems = {e: stack.enter_context(nc.semaphore(f"s_{e}")) for e in self.ENGINES}
        block = stack.enter_context(nc.Block())
        ops = self.ops

        def emit_for(name):
            def emitf(eng):
                done = {}
                for (enm, meth, args, kw, waits, inc) in ops:
                    if enm != name:
                        continue
                    for (we, wv) in waits:
                        if done.get(we, -1) < wv:
                            eng.wait_ge(sems[we], wv)
                            done[we] = wv
                    getattr(eng, meth)(*args, **kw).then_inc(sems[enm], inc)
                for (we, wv) in final:
                    if we != name or name in Prog.DMA_ENG:
                        eng.wait_ge(sems[we], wv)
            return emitf
        block.sync(emit_for("sync"))
        block.gpsimd(emit_for("gpsimd"))
        block.vector(emit_for("vector"))
        block.scalar(emit_for("scalar"))
        block.tensor(emit_for("tensor"))
        stack.close()


DEBUG = False
SERIAL = False

def build_program():
    nc = bass.Bass("TRN2", target_bir_lowering=False)
    P = Prog(nc)
    E = P.emit

    def din(name, shape, dt=F32):
        return nc.dram_tensor(name, shape, dt, kind="ExternalInput").ap()
    x_all = din("x_all", [L, D])
    x_own = din("x_own", [LQ, D])
    w_qkv = din("w_qkv", [3 * D, D])
    w_out = din("w_out", [D, D])
    w_gate = din("w_gate", [FI, D])
    w_up = din("w_up", [FI, D])
    w_down = din("w_down", [D, FI])
    cos_f = din("cos_f", [128, L])
    sin_f = din("sin_f", [128, L])
    cos_o = din("cos_o", [128, LQ])
    sin_o = din("sin_o", [128, LQ])
    ident = din("ident", [128, 128], BF16)
    out = nc.dram_tensor("out", [LQ, D], F32, kind="ExternalOutput").ap()

    scrkind = "ExternalOutput" if DEBUG else "Internal"
    kscr = nc.dram_tensor("kscr", [8, 128, L], BF16, kind=scrkind).ap()
    qscr = nc.dram_tensor("qscr", [H, 64, LQ], BF16, kind=scrkind).ap()
    mscr = nc.dram_tensor("mscr", [H, LQ], BF16, kind=scrkind).ap()
    vscr = nc.dram_tensor("vscr", [8, 32, 128, VPW], BF16, kind=scrkind).ap()
    wgscr = nc.dram_tensor("wgscr", [8, 128, FI], BF16, kind=scrkind).ap()
    wuscr = nc.dram_tensor("wuscr", [8, 128, FI], BF16, kind=scrkind).ap()
    wdscr = nc.dram_tensor("wdscr", [FI // 128, 128, D], BF16, kind=scrkind).ap()
    if DEBUG:
        dbg_xlnf = nc.dram_tensor("dbg_xlnf", [128, 4096], BF16, kind="ExternalOutput").ap()
        dbg_xs = nc.dram_tensor("dbg_xs", [128, D], F32, kind="ExternalOutput").ap()
        dbg_xc = nc.dram_tensor("dbg_xc", [128, D], F32, kind="ExternalOutput").ap()
        dbg_xlnt = nc.dram_tensor("dbg_xlnt", [128, D], BF16, kind="ExternalOutput").ap()
        dbg_id = nc.dram_tensor("dbg_id", [128, 128], BF16, kind="ExternalOutput").ap()
        dbg_lnv = nc.dram_tensor("dbg_lnv", [128, 2], F32, kind="ExternalOutput").ap()
        dbg_kpre = nc.dram_tensor("dbg_kpre", [2, 128, 512], F32, kind="ExternalOutput").ap()
        dbg_q = nc.dram_tensor("dbg_q", [8, 128, LQ], BF16, kind="ExternalOutput").ap()
        dbg_kp = nc.dram_tensor("dbg_kp", [128, L], BF16, kind="ExternalOutput").ap()
        dbg_vp = nc.dram_tensor("dbg_vp", [128, 32 * VPW], BF16, kind="ExternalOutput").ap()
        dbg_pt = nc.dram_tensor("dbg_pt", [128, 2048], BF16, kind="ExternalOutput").ap()
        dbg_av = nc.dram_tensor("dbg_av", [64, 512], F32, kind="ExternalOutput").ap()
        dbg_avd = nc.dram_tensor("dbg_avd", [1, 512], F32, kind="ExternalOutput").ap()
        dbg_st = nc.dram_tensor("dbg_st", [128, 2048], F32, kind="ExternalOutput").ap()
        dbg_hb = nc.dram_tensor("dbg_hb", [128, 16 * ICH], F32, kind="ExternalOutput").ap()
        dbg_hq = nc.dram_tensor("dbg_hq", [128, 16 * ICH], BF16, kind="ExternalOutput").ap()
        dbg_wg = nc.dram_tensor("dbg_wg", [128, ICH], BF16, kind="ExternalOutput").ap()
        dbg_gt = nc.dram_tensor("dbg_gt", [128, ICH], F32, kind="ExternalOutput").ap()
        dbg_y15 = nc.dram_tensor("dbg_y15", [4, 128, D], F32, kind="ExternalOutput").ap()
        dbg_hqt = nc.dram_tensor("dbg_hqt", [128, LQ], BF16, kind="ExternalOutput").ap()
        dbg_attn = nc.dram_tensor("dbg_attn", [8, 128, LQ], BF16, kind="ExternalOutput").ap()
        dbg_bias = nc.dram_tensor("dbg_bias", [1, H], F32, kind="ExternalOutput").ap()
        dbg_y1 = nc.dram_tensor("dbg_y1", [16, 128, D], F32, kind="ExternalOutput").ap()
        dbg_xq = nc.dram_tensor("dbg_xq", [8, 128, LQ], BF16, kind="ExternalOutput").ap()

    def sb(name, shape, dt):
        return nc.alloc_sbuf_tensor(name, shape, dt)

    s_id = sb("s_id", [128, 128], BF16)
    ones2 = sb("ones2", [128, 2], BF16)
    onesr = sb("onesr", [1, 128], F32)
    mbias = sb("mbias", [128, 7], F32)
    mq2 = sb("mq2", [1, H], F32)
    mk2 = sb("mk2", [1, H], F32)
    redt = sb("redt", [1, 1], F32)
    bsc = sb("bsc", [1, H], F32)
    bsc560 = sb("bsc560", [1, H], F32)
    mtmp = sb("mtmp", [1, 512], F32)
    mneg = sb("mneg", [1, 512], BF16)
    ln_m = sb("ln_m", [128, 1], F32)
    ln_v = sb("ln_v", [128, 1], F32)
    ln_r = sb("ln_r", [128, 1], F32)
    wnat = [sb(f"wnat{j}", [128, D], BF16) for j in range(2)]

    base0 = nc.sbuf_base
    base = (base0 + 63) & ~63
    cursor = [base]
    placements = {}

    def place(region, name, shape, dt):
        t = nc.alloc_sbuf_tensor_at(name, shape, dt, offset=placements[region][1])
        sz = int(np.prod(shape[1:])) * mybir.dt.size(dt) if len(shape) > 1 else mybir.dt.size(dt)
        sz = shape[-1] * mybir.dt.size(dt)
        placements[region][1] += (sz + 63) & ~63
        assert placements[region][1] <= placements[region][0] + placements[region][2], \
            f"region {region} overflow at {name}"
        return t

    def region(name, size):
        placements[name] = [cursor[0], cursor[0], size]
        cursor[0] += size

    def reset_region(name):
        placements[name][1] = placements[name][0]

    KB = 1024
    region("R1", 32 * KB)
    region("R2", 34 * KB)
    region("R3", 93 * KB)
    region("R4", 38 * KB)
    assert cursor[0] <= nc.sbuf_top - 64, f"sbuf over: {cursor[0]} > {nc.sbuf_top}"

    qfm = [place("R1", f"qfm{j}", [128, LQ], BF16) for j in range(8)]
    reset_region("R1")
    xqfm = [place("R1", f"xqfm{j}", [128, LQ], BF16) for j in range(8)]
    attn = [place("R2", f"attn{j}", [128, LQ], BF16) for j in range(8)]
    reset_region("R2")
    hbuf = place("R2", "hbuf", [128, 16 * ICH], F32)
    hqtm = place("R2", "hqtm", [128, 16 * ICH], BF16)
    hqT = [place("R2", f"hqT{j}", [128, LQ], BF16) for j in range(2)]
    silu_t = place("R2", "silu_t", [128, ICH], F32)
    wT = [place("R3", f"wT{j}", [128, 3 * D], BF16) for j in range(8)]
    xs = [place("R3", f"xs{j}", [128, D], F32) for j in range(2)]
    xc = place("R3", "xc", [128, D], F32)
    sqt = place("R3", "sqt", [128, D], BF16)
    xlnt = place("R3", "xlnt", [128, D], BF16)
    xlnf = place("R3", "xlnf", [128, 8 * 512], BF16)
    cosw = [place("R3", f"cosw{j}", [128, 512], F32) for j in range(2)]
    sinw = [place("R3", f"sinw{j}", [128, 512], F32) for j in range(2)]
    rt1 = place("R3", "rt1", [128, 512], F32)
    rt2 = place("R3", "rt2", [128, 512], F32)
    kq_pre = place("R3", "kq_pre", [128, 512], BF16)
    sq2 = place("R3", "sq2", [128, 512], BF16)
    vt = place("R3", "vt", [128, 8 * VPW], BF16)
    A_KEYS = [f"wT{j}" for j in range(8)] + \
        ["xs0", "xs1", "xc", "sqt", "xlnt", "xlnf", "cosw0", "cosw1", "sinw0", "sinw1",
         "rt1", "rt2", "kq_pre", "sq2", "vt"]
    reset_region("R3")
    kax = [place("R3", f"kA{j}", [66, L], BF16) for j in range(2)]
    kbx = [place("R3", f"kB{j}", [66, L], BF16) for j in range(2)]
    qx = [place("R3", f"qx{j}", [66, LQ], BF16) for j in range(2)]
    vpair = [place("R3", f"vpair{j}", [128, 32 * VPW], BF16) for j in range(2)]
    pt = [place("R3", f"pt{j}", [128, 2048], BF16) for j in range(2)]
    rdrow = place("R3", "rdrow", [1, 512], F32)
    rdb = place("R3", "rdb", [64, 512], F32)
    wstr = [place("R3", "wstr0", [128, FI], F32)] * 2
    wq_stage = place("R3", "wq_stage", [128, FI], BF16)
    B_KEYS = ["kA0", "kA1", "kB0", "kB1", "qx0", "qx1", "vpair0", "vpair1",
              "pt0", "pt1", "rdrow", "rdb", "wstr0", "wq_stage"]
    reset_region("R3")
    y1 = [place("R3", f"y1_{t}", [128, D], F32) for t in range(16)]
    woutT = [place("R3", f"woutT{j}", [128, D], BF16) for j in range(8)]
    xln2 = place("R3", "xln2", [128, D], F32)
    resid = [place("R3", f"resid{j}", [128, D], F32) for j in range(2)]
    C_KEYS = [f"y1_{t}" for t in range(16)] + [f"woutT{j}" for j in range(8)] + \
        ["xln2", "resid0", "resid1"]
    placements["R3"][1] = placements["R3"][0] + 64 * KB
    wgqT = [[place("R3", f"wgqT{p}_{j}", [128, ICH], BF16) for j in range(8)] for p in range(2)]
    wuqT = [[place("R3", f"wuqT{p}_{j}", [128, ICH], BF16) for j in range(8)] for p in range(2)]
    wdqT = [[place("R3", f"wdqT{p}_{j}", [128, D], BF16) for j in range(2)] for p in range(2)]
    D_KEYS = [f"wgqT{p}_{j}" for p in range(2) for j in range(8)] + \
        [f"wuqT{p}_{j}" for p in range(2) for j in range(8)] + \
        [f"wdqT{p}_{j}" for p in range(2) for j in range(2)]
    q_ax = place("R4", "q_ax", [128, 2048], F32)
    q_a = place("R4", "q_a", [128, 2048], F32)
    q_sg = place("R4", "q_sg", [128, 2048], F32)
    q_t0 = place("R4", "q_t0", [128, 2048], BF16)
    q_t1 = place("R4", "q_t1", [128, 2048], BF16)
    q_bm = place("R4", "q_bm", [128, 128], F32)
    q_sc = place("R4", "q_sc", [128, 128], F32)
    q_rs = place("R4", "q_rs", [128, 128], F32)
    q_out = place("R4", "q_out", [128, 2048], BF16)

    P4 = nc.alloc_psum_tensor("P4", [128, 2048], F32)
    PB4 = nc.alloc_psum_tensor("PB4", [128, 512], F32)
    PB5 = nc.alloc_psum_tensor("PB5", [128, 512], F32)
    PB6 = nc.alloc_psum_tensor("PB6", [128, 512], F32)
    PB7 = nc.alloc_psum_tensor("PB7", [128, 512], F32)

    E("sync", "dma_start", s_id[:], ident, w=["id"])
    E("vector", "memset", ones2[:, :], 0.0, w=["ones2"])
    E("vector", "memset", ones2[0:64, 0:1], 1.0, r=["ones2"], w=["ones2"])
    E("vector", "memset", ones2[64:128, 1:2], 1.0, r=["ones2"], w=["ones2"])
    E("vector", "memset", onesr[:, :], 1.0, w=["onesr"])
    for i, m in enumerate([0.25, 0.75, 1.25, 1.75, 2.5, 3.5, 5.0]):
        E("vector", "memset", mbias[:, i:i + 1], -m, w=["mbias"])
    E("vector", "memset", mq2[:, :], 0.0, w=["mq2"])
    E("vector", "memset", mk2[:, :], 0.0, w=["mk2"])

    def transpose8(src_bf_ap, src_keys, store):
        for g in range(2):
            for k in range(4):
                dc = g * 4 + k
                E("tensor", "matmul", P4[:, k * 128:(k + 1) * 128],
                  src_bf_ap[:, dc * 128:(dc + 1) * 128], s_id[:],
                  start=True, stop=True, r=list(src_keys) + ["id"], w=["P4"])
            for k in range(4):
                store(g * 4 + k, P4[:, k * 128:(k + 1) * 128], "P4")

    def build_wT(tiles, dst_col_of):
        for t in tiles:
            wn = wnat[t % 2]
            wkey = f"wnat{t % 2}"
            E("gpsimd", "dma_start", wn[:], w_qkv[t * 128:(t + 1) * 128, :], w=[wkey])
            col = dst_col_of(t)
            def st(dc, ps, pk, _c=col):
                E("vector", "tensor_copy", wT[dc][:, _c:_c + 128], ps,
                  r=[pk], w=[f"wT{dc}"])
            transpose8(wn.ap(), [wkey], st)

    def build_rot():
        for dc in range(8):
            src = wT[dc].ap()
            s3 = src[:, 0:D].rearrange("p (h t) -> p h t", t=64)
            d3 = src[:, D:2 * D].rearrange("p (h t) -> p h t", t=64)
            E("vector", "tensor_scalar", d3[:, :, 0:32], s3[:, :, 32:64], -1.0, None,
              Alu.mult, r=[f"wT{dc}"], w=[f"wT{dc}"])
            E("vector", "tensor_copy", d3[:, :, 32:64], s3[:, :, 0:32],
              r=[f"wT{dc}"], w=[f"wT{dc}"])

    def ln_tile(src_ap, src_keys, dst_ap, dst_key, xc_ap=None, xc_key="xc",
                sq_ap=None, sq_key="sqt"):
        if xc_ap is None:
            xc_ap = xc[:]
        if sq_ap is None:
            sq_ap = sqt[:]
        E("vector", "tensor_reduce", ln_m[:], src_ap, AX.X, Alu.add, r=src_keys, w=["ln_m"])
        E("vector", "tensor_scalar", ln_m[:], ln_m[:], 1.0 / D, None, Alu.mult,
          r=["ln_m"], w=["ln_m"])
        E("vector", "tensor_scalar", xc_ap, src_ap, ln_m[:], None, Alu.subtract,
          r=src_keys + ["ln_m"], w=[xc_key])
        E("scalar", "activation", sq_ap, xc_ap, Act.Square, accum_out=ln_v[:],
          r=[xc_key], w=[sq_key, "ln_v"])
        E("vector", "tensor_scalar", ln_v[:], ln_v[:], 1.0 / D, 1e-5, Alu.mult, Alu.add,
          r=["ln_v"], w=["ln_v"])
        E("scalar", "activation", ln_v[:], ln_v[:], Act.Sqrt, r=["ln_v"], w=["ln_v"])
        E("vector", "reciprocal", ln_r[:], ln_v[:], r=["ln_v"], w=["ln_r"])
        E("vector", "tensor_scalar", dst_ap, xc_ap, ln_r[:], None, Alu.mult,
          r=[xc_key, "ln_r"], w=[dst_key])

    def build_xlnf(x_src, tok0_base):
        for t in range(4):
            tok0 = tok0_base + t * 128
            xsb = xs[t % 2]
            xkey = f"xs{t % 2}"
            E("gpsimd", "dma_start", xsb[:], x_src[tok0:tok0 + 128, :], w=[xkey])
            ln_tile(xsb[:], [xkey], xlnt[:], "xlnt")
            if DEBUG and tok0_base == 0 and t == 0 and x_src is x_all:
                E("sync", "dma_start", dbg_xs, xsb[:], r=[xkey])
                E("sync", "dma_start", dbg_xc, xc[:], r=["xc"])
                E("sync", "dma_start", dbg_xlnt, xlnt[:], r=["xlnt"])
                E("sync", "dma_start", dbg_id, s_id[:], r=["id"])
                E("sync", "dma_start", dbg_lnv[:, 0:1], ln_v[:], r=["ln_v"],
                  allow_slow_non_contiguous=True)
                E("sync", "dma_start", dbg_lnv[:, 1:2], ln_r[:], r=["ln_r"],
                  allow_slow_non_contiguous=True)
            def stx(dc, ps, pk, _t=t):
                E("vector", "tensor_copy",
                  xlnf[:, dc * 512 + _t * 128: dc * 512 + (_t + 1) * 128], ps,
                  r=[pk], w=["xlnf"])
            transpose8(xlnt.ap(), ["xlnt"], stx)

    def gen_rope(fc, cw_ap, sw_ap, cw_keys, out_ap, out_key, m2t, m2key, dbg=None):
        for dc in range(8):
            E("tensor", "matmul", PB4[:, :], wT[dc][:, fc * 128:(fc + 1) * 128],
              xlnf[:, dc * 512:(dc + 1) * 512], start=(dc == 0), stop=(dc == 7),
              r=[f"wT{dc}", "xlnf"], w=["PB4"])
        for dc in range(8):
            E("tensor", "matmul", PB5[:, :], wT[dc][:, D + fc * 128: D + (fc + 1) * 128],
              xlnf[:, dc * 512:(dc + 1) * 512], start=(dc == 0), stop=(dc == 7),
              r=[f"wT{dc}", "xlnf"], w=["PB5"])
        if dbg is not None:
            E("vector", "tensor_copy", rt1[:], PB4[:, :], r=["PB4"], w=["rt1"])
            E("sync", "dma_start", dbg[0], rt1[:], r=["rt1"])
            E("vector", "tensor_copy", rt1[:], PB5[:, :], r=["PB5"], w=["rt1"])
            E("sync", "dma_start", dbg[1], rt1[:], r=["rt1"])
        E("vector", "tensor_tensor", rt1[:], PB4[:, :], cw_ap, Alu.mult,
          r=["PB4"] + cw_keys, w=["rt1"])
        E("vector", "tensor_tensor", rt2[:], PB5[:, :], sw_ap, Alu.mult,
          r=["PB5"] + cw_keys, w=["rt2"])
        E("vector", "tensor_tensor", out_ap, rt1[:], rt2[:], Alu.add,
          r=["rt1", "rt2"], w=[out_key])
        E("vector", "tensor_tensor", sq2[:], out_ap, out_ap, Alu.mult,
          r=[out_key], w=["sq2"])
        for par2 in range(2):
            pnb = PB6 if par2 == 0 else PB7
            pnk = "PB6" if par2 == 0 else "PB7"
            E("tensor", "matmul", pnb[0:1, :], ones2[:, par2:par2 + 1], sq2[:],
              start=True, stop=True, r=["ones2", "sq2"], w=[pnk])
        for par2 in range(2):
            hd = 2 * fc + par2
            pnb = PB6 if par2 == 0 else PB7
            pnk = "PB6" if par2 == 0 else "PB7"
            E("vector", "tensor_reduce", redt[:], pnb[0:1, :], AX.X, Alu.max,
              r=[pnk], w=["redt"])
            E("vector", "tensor_tensor", m2t[0:1, hd:hd + 1], m2t[0:1, hd:hd + 1],
              redt[:], Alu.max, r=["redt", m2key], w=[m2key])

    build_wT(range(8, 16), lambda t: (t - 8) * 128)
    build_wT(range(16, 24), lambda t: 2 * D + (t - 16) * 128)
    build_rot()
    for c in range(8):
        cw0 = c * 512
        par = c % 2
        cwb, swb = cosw[par], sinw[par]
        E("sync", "dma_start", cwb[:], cos_f[:, cw0:cw0 + 512], w=[f"cosw{par}"])
        E("sync", "dma_start", swb[:], sin_f[:, cw0:cw0 + 512], w=[f"sinw{par}"])
        build_xlnf(x_all, cw0)
        if DEBUG and c == 0:
            E("sync", "dma_start", dbg_xlnf, xlnf[:], r=["xlnf"])
        for fc in range(8):
            gen_rope(fc, cwb[:], swb[:], [f"cosw{par}", f"sinw{par}"],
                     kq_pre[:], "kq_pre", mk2, "mk2",
                     dbg=(dbg_kpre if (DEBUG and c == 0 and fc == 0) else None))
            E("sync", "dma_start", kscr[fc, :, cw0:cw0 + 512], kq_pre[:],
              r=["kq_pre"], w=[f"kscr{fc}"])
        for t in range(4):
            for half in range(2):
                pb = PB7 if half == 0 else PB4
                pk = "PB7" if half == 0 else "PB4"
                for dc in range(8):
                    E("tensor", "matmul", pb[:, :],
                      xlnf[:, dc * 512 + t * 128: dc * 512 + (t + 1) * 128],
                      wT[dc][:, 2 * D + half * 512: 2 * D + (half + 1) * 512],
                      start=(dc == 0), stop=(dc == 7),
                      r=[f"wT{dc}", "xlnf"], w=[pk])
            v4 = vt.ap().rearrange("p (h w) -> p h w", w=66)
            E("vector", "memset", v4[:, :, 64:66], 0.0, w=["vt"])
            E("vector", "memset", v4[:, :, 64:65], 1.0, r=["vt"], w=["vt"])
            for half in range(2):
                pb = PB7 if half == 0 else PB4
                pk = "PB7" if half == 0 else "PB4"
                srcv = pb.ap().rearrange("p (hh w) -> p hh w", w=64)
                base_c = 4 * half * VPW
                dstv = vt.ap()[:, base_c: base_c + 4 * VPW].rearrange(
                    "p (hh w) -> p hh w", w=66)[:, :, 0:64]
                E("vector", "tensor_copy", dstv, srcv, r=[pk], w=["vt"])
            for pr in range(8):
                E("gpsimd", "dma_start", vscr[pr, c * 4 + t, :, :],
                  vt[:, pr * VPW:(pr + 1) * VPW], r=["vt"], w=[f"vscr{pr}"])
    build_wT(range(0, 8), lambda t: t * 128)
    build_rot()
    for c in range(4):
        cw0 = c * 512
        par = c % 2
        cwb, swb = cosw[par], sinw[par]
        E("sync", "dma_start", cwb[:], cos_o[:, cw0:cw0 + 512], w=[f"cosw{par}"])
        E("sync", "dma_start", swb[:], sin_o[:, cw0:cw0 + 512], w=[f"sinw{par}"])
        build_xlnf(x_own, cw0)
        for fc in range(8):
            gen_rope(fc, cwb[:], swb[:], [f"cosw{par}", f"sinw{par}"],
                     qfm[fc][:, cw0:cw0 + 512], f"qfm{fc}", mq2, "mq2")
            E("sync", "dma_start", qscr[2 * fc, :, cw0:cw0 + 512],
              qfm[fc][0:64, cw0:cw0 + 512], r=[f"qfm{fc}"], w=["qscr"])
            E("sync", "dma_start", qscr[2 * fc + 1, :, cw0:cw0 + 512],
              qfm[fc][64:128, cw0:cw0 + 512], r=[f"qfm{fc}"], w=["qscr"])

    E("vector", "tensor_tensor", bsc[0:1, :], mq2[0:1, :], mk2[0:1, :], Alu.mult,
      r=["mq2", "mk2"], w=["bsc"])
    E("scalar", "activation", bsc[0:1, :], bsc[0:1, :], Act.Sqrt, r=["bsc"], w=["bsc"])
    E("vector", "tensor_scalar", bsc560[0:1, :], bsc[0:1, :], 1.02, -560.0,
      Alu.mult, Alu.add, r=["bsc"], w=["bsc560"])

    build_wT(range(8, 16), lambda t: (t - 8) * 128)
    build_rot()
    for c in range(4):
        cw0 = c * 512
        par = c % 2
        cwb, swb = cosw[par], sinw[par]
        E("sync", "dma_start", cwb[:], cos_o[:, cw0:cw0 + 512], w=[f"cosw{par}"])
        E("sync", "dma_start", swb[:], sin_o[:, cw0:cw0 + 512], w=[f"sinw{par}"])
        build_xlnf(x_own, cw0)
        for fc in range(8):
            for dc in range(8):
                E("tensor", "matmul", PB4[:, :], wT[dc][:, fc * 128:(fc + 1) * 128],
                  xlnf[:, dc * 512:(dc + 1) * 512], start=(dc == 0), stop=(dc == 7),
                  r=[f"wT{dc}", "xlnf"], w=["PB4"])
            for dc in range(8):
                E("tensor", "matmul", PB5[:, :], wT[dc][:, D + fc * 128: D + (fc + 1) * 128],
                  xlnf[:, dc * 512:(dc + 1) * 512], start=(dc == 0), stop=(dc == 7),
                  r=[f"wT{dc}", "xlnf"], w=["PB5"])
            E("vector", "tensor_tensor", rt1[:], PB4[:, :], cwb[:], Alu.mult,
              r=["PB4", f"cosw{par}"], w=["rt1"])
            E("vector", "tensor_tensor", rt2[:], PB5[:, :], swb[:], Alu.mult,
              r=["PB5", f"sinw{par}"], w=["rt2"])
            E("vector", "tensor_tensor", kq_pre[:], rt1[:], rt2[:], Alu.add,
              r=["rt1", "rt2"], w=["kq_pre"])
            E("vector", "tensor_tensor", sq2[:], qfm[fc][:, cw0:cw0 + 512], kq_pre[:],
              Alu.mult, r=[f"qfm{fc}", "kq_pre"], w=["sq2"])
            for par2 in range(2):
                h = 2 * fc + par2
                E("tensor", "matmul", PB6[0:1, :], ones2[:, par2:par2 + 1], sq2[:],
                  start=True, stop=True, r=["ones2", "sq2"], w=["PB6"])
                E("vector", "tensor_scalar", mtmp[:], PB6[0:1, :],
                  bsc560[0:1, h:h + 1], None, Alu.max, r=["PB6", "bsc560"], w=["mtmp"])
                E("vector", "tensor_scalar", mneg[:], mtmp[:], -1.0, None, Alu.mult,
                  r=["mtmp"], w=["mneg"])
                E("sync", "dma_start", mscr[h, cw0:cw0 + 512], mneg[:],
                  r=["mneg"], w=["mscr"])

    MIDS = [0.25, 0.75, 1.25, 1.75, 2.5, 3.5, 5.0]
    STEP = [0.5, 0.5, 0.5, 0.5, 1.0, 1.0, 2.0]

    def fp4_quant(src_ap, src_keys, n, out_ap, out_key, ladder="vector"):
        assert n <= 2048 and n % 16 == 0
        nb = n // 16
        E("vector", "tensor_scalar", q_sg[:, 0:n], src_ap, 0.0, 2.0, Alu.is_ge, Alu.mult,
          r=src_keys, w=["q_sg"])
        E("vector", "scalar_tensor_tensor", q_ax[:, 0:n], q_sg[:, 0:n], 1.0, src_ap,
          Alu.subtract, Alu.mult, r=["q_sg"] + src_keys, w=["q_ax"])
        ax3 = q_ax.ap()[:, 0:n].rearrange("p (b k) -> p b k", k=16)
        E("vector", "tensor_reduce", q_bm[:, 0:nb], ax3, AX.X, Alu.max,
          r=["q_ax"], w=["q_bm"])
        E("vector", "tensor_scalar", q_sc[:, 0:nb], q_bm[:, 0:nb], 1.0 / 6.0, 1e-12,
          Alu.mult, Alu.max, r=["q_bm"], w=["q_sc"])
        E("vector", "reciprocal", q_rs[:, 0:nb], q_sc[:, 0:nb], r=["q_sc"], w=["q_rs"])
        rs3 = q_rs.ap()[:, 0:nb].unsqueeze(2).broadcast_to([128, nb, 16])
        a3 = q_a.ap()[:, 0:n].rearrange("p (b k) -> p b k", k=16)
        E("vector", "tensor_tensor", a3, ax3, rs3, Alu.mult, r=["q_ax", "q_rs"], w=["q_a"])
        if ladder == "vector":
            E("vector", "tensor_scalar", q_t0[:, 0:n], q_a[:, 0:n], MIDS[0], STEP[0],
              Alu.is_gt, Alu.mult, r=["q_a"], w=["q_t0"])
            for i in range(1, 7):
                E("vector", "tensor_scalar", q_t1[:, 0:n], q_a[:, 0:n], MIDS[i], STEP[i],
                  Alu.is_gt, Alu.mult, r=["q_a"], w=["q_t1"])
                E("vector", "tensor_tensor", q_t0[:, 0:n], q_t0[:, 0:n], q_t1[:, 0:n],
                  Alu.add, r=["q_t1", "q_t0"], w=["q_t0"])
            E("vector", "tensor_scalar", q_out[:, 0:n], q_sg[:, 0:n], 1.0, None,
              Alu.subtract, r=["q_sg"], w=["q_outs"])
            E("vector", "tensor_tensor", q_t1[:, 0:n], q_t0[:, 0:n], q_out[:, 0:n],
              Alu.mult, r=["q_t0", "q_outs"], w=["q_t1"])
        else:
            for i in range(7):
                E("scalar", "activation", q_out[:, 0:n], q_a[:, 0:n], Act.Sign,
                  bias=mbias[:, i:i + 1], r=["q_a", "mbias"], w=["q_outs"])
                if i == 0:
                    E("vector", "tensor_scalar", q_t0[:, 0:n], q_out[:, 0:n],
                      STEP[0] / 2.0, None, Alu.mult, r=["q_outs"], w=["q_t0"])
                else:
                    E("vector", "scalar_tensor_tensor", q_t0[:, 0:n], q_out[:, 0:n],
                      STEP[i] / 2.0, q_t0[:, 0:n], Alu.mult, Alu.add,
                      r=["q_outs", "q_t0"], w=["q_t0"])
            E("vector", "tensor_scalar", q_t0[:, 0:n], q_t0[:, 0:n], 3.0, None,
              Alu.add, r=["q_t0"], w=["q_t0"])
            E("vector", "tensor_scalar", q_out[:, 0:n], q_sg[:, 0:n], 1.0, None,
              Alu.subtract, r=["q_sg"], w=["q_outs"])
            E("vector", "tensor_tensor", q_t1[:, 0:n], q_t0[:, 0:n], q_out[:, 0:n],
              Alu.mult, r=["q_t0", "q_outs"], w=["q_t1"])
        sc3 = q_sc.ap()[:, 0:nb].unsqueeze(2).broadcast_to([128, nb, 16])
        st3 = q_t1.ap()[:, 0:n].rearrange("p (b k) -> p b k", k=16)
        o3 = out_ap.rearrange("p (b k) -> p b k", k=16)
        E("vector", "tensor_tensor", o3, st3, sc3, Alu.mult,
          r=["q_t1", "q_sc"], w=[out_key])

    wtiles = [("g", i) for i in range(FI // 128)] + \
             [("u", i) for i in range(FI // 128)] + \
             [("d", i) for i in range(D // 128)]

    def emit_wtile(kind, ti):
        wn = wstr[0]
        wkey = "wstr0"
        if kind == "g":
            src, n, scr, skey = w_gate[ti * 128:(ti + 1) * 128, :], D, wgscr, "wgscr"
        elif kind == "u":
            src, n, scr, skey = w_up[ti * 128:(ti + 1) * 128, :], D, wuscr, "wuscr"
        else:
            src, n, scr, skey = w_down[ti * 128:(ti + 1) * 128, :], FI, wdscr, "wdscr"
        E("sync", "dma_start", wn[:, 0:n], src, w=[wkey])
        pos = 0
        while pos < n:
            seg = min(2048, n - pos)
            fp4_quant(wn.ap()[:, pos:pos + seg], [wkey], seg,
                      wq_stage.ap()[:, pos:pos + seg], "wq_stage", ladder="vector")
            pos += seg
        nblk = n // 128
        for g in range((nblk + 3) // 4):
            lo, hi = g * 4, min(g * 4 + 4, nblk)
            for k in range(lo, hi):
                E("tensor", "matmul", PB4[:, (k - lo) * 128:(k - lo + 1) * 128],
                  wq_stage[:, k * 128:(k + 1) * 128], s_id[:], start=True, stop=True,
                  r=["wq_stage", "id"], w=["PB4"])
            for k in range(lo, hi):
                dst = scr[k, :, ti * 128:(ti + 1) * 128] if kind == "d" else \
                    scr[k, :, ti * 128:(ti + 1) * 128]
                E("vector", "tensor_copy", q_out[:, (k % 16) * 128:((k % 16) + 1) * 128],
                  PB4[:, (k - lo) * 128:(k - lo + 1) * 128], r=["PB4"], w=["q_out"])
                E("sync", "dma_start", dst,
                  q_out[:, (k % 16) * 128:((k % 16) + 1) * 128], r=["q_out"], w=[skey])

    if DEBUG:
        for j in range(8):
            E("sync", "dma_start", dbg_q[j], qfm[j][:], r=[f"qfm{j}"])
        E("sync", "dma_start", dbg_bias[0:1, :], bsc[0:1, :], r=["bsc"])

    P.alias_seed(B_KEYS, A_KEYS)

    for j in range(2):
        E("vector", "memset", kax[j][64:65, :], 1.0, w=[f"kA{j}"])
        E("vector", "memset", kbx[j][64:65, :], 1.0, w=[f"kB{j}"])

    def load_pair(pr):
        vb = vpair[pr % 2]
        E("sync", "dma_start", kax[pr % 2][0:64, :], kscr[pr, 0:64, :],
          r=[f"kscr{j}" for j in range(8)], w=[f"kA{pr % 2}"])
        E("sync", "dma_start", kbx[pr % 2][0:64, :], kscr[pr, 64:128, :],
          r=[f"kscr{j}" for j in range(8)], w=[f"kB{pr % 2}"])
        for seg in range(4):
            srcv = vscr[pr, seg * 8:(seg + 1) * 8, :, :].transpose([1, 0, 2])
            dstv = vb.ap()[:, seg * 8 * VPW:(seg + 1) * 8 * VPW].rearrange(
                "p (c w) -> p c w", w=VPW)
            E("gpsimd", "dma_start", dstv, srcv,
              r=[f"vscr{pr}"], w=[f"vpair{pr % 2}"])

    def load_q(h):
        qb = qx[h % 2]
        E("sync", "dma_start", qb[0:64, :], qscr[h, :, :], r=["qscr"], w=[f"qx{h % 2}"])
        E("sync", "dma_start", qb[64:65, :], mscr[h:h + 1, :], r=["mscr"], w=[f"qx{h % 2}"])

    NW = len(wtiles)
    wpos = 0
    load_pair(0)
    load_q(0)
    for pr in range(8):
        if pr < 7:
            load_pair(pr + 1)
        vkey = f"vpair{pr % 2}"
        vb = vpair[pr % 2]
        if DEBUG and pr == 0:
            E("sync", "dma_start", dbg_kp[0:64, :], kax[0][0:64, :], r=["kA0"])
            E("sync", "dma_start", dbg_kp[64:128, :], kbx[0][0:64, :], r=["kB0"])
            E("sync", "dma_start", dbg_vp, vb[:], r=[vkey])
        for hh in range(2):
            h = 2 * pr + hh
            if h < H - 1:
                load_q(h + 1)
            off = hh * 64
            kx = kax[pr % 2] if hh == 0 else kbx[pr % 2]
            kkey = (f"kA{pr % 2}" if hh == 0 else f"kB{pr % 2}")
            qb = qx[h % 2]
            qkey = f"qx{h % 2}"
            for qc in range(4):
                q0 = qc * 512

                def emit_qk(g):
                    half = g % 2
                    for j in range(2):
                        kc = 2 * g + j
                        E("tensor", "matmul",
                          P4[:, half * 1024 + j * 512: half * 1024 + (j + 1) * 512],
                          kx[0:65, kc * 128:(kc + 1) * 128],
                          qb[0:65, q0:q0 + 512],
                          start=True, stop=True, r=[kkey, qkey], w=[f"P4h{half}"])

                def emit_exp(g):
                    half = g % 2
                    E("scalar", "activation", pt[half][:, 0:1024],
                      P4[:, half * 1024:(half + 1) * 1024], Act.Exp, scale=0.125,
                      r=[f"P4h{half}"], w=[f"pt{half}"])

                def emit_av(g):
                    half = g % 2
                    for j in range(2):
                        kc = 2 * g + j
                        vsl = vb.ap()[:, kc * VPW + hh * 66: kc * VPW + hh * 66 + 65]
                        E("tensor", "matmul", PB4[0:65, :], vsl,
                          pt[half][:, j * 512:(j + 1) * 512],
                          start=(kc == 0), stop=(kc == 31),
                          r=[vkey, f"pt{half}"], w=["PB4"])

                emit_qk(0)
                emit_exp(0)
                for g in range(1, 16):
                    emit_qk(g)
                    emit_exp(g)
                    emit_av(g - 1)
                emit_av(15)
                if DEBUG and pr == 0 and hh == 0 and qc == 0:
                    E("vector", "tensor_copy", rdb[:], PB4[0:64, :], r=["PB4"], w=["rdb"])
                    E("sync", "dma_start", dbg_av, rdb[:], r=["rdb"])
                    E("vector", "tensor_copy", rdb[0:1, :], PB4[64:65, :], r=["PB4"], w=["rdb"])
                    E("sync", "dma_start", dbg_avd, rdb[0:1, :], r=["rdb"])
                E("vector", "reciprocal", rdrow[:], PB4[64:65, :], r=["PB4"], w=["rdrow"])
                E("tensor", "matmul", PB5[0:64, :], onesr[0:1, 0:64], rdrow[:],
                  start=True, stop=True, r=["onesr", "rdrow"], w=["PB5"])
                E("vector", "tensor_copy", rdb[:], PB5[0:64, :], r=["PB5"], w=["rdb"])
                E("vector", "tensor_tensor", attn[pr][off:off + 64, q0:q0 + 512],
                  PB4[0:64, :], rdb[:], Alu.mult, r=["PB4", "rdb"], w=[f"attn{pr}"])
        wend = (pr + 1) * NW // 8
        while wpos < wend:
            emit_wtile(*wtiles[wpos])
            wpos += 1

    if DEBUG:
        for j in range(8):
            E("sync", "dma_start", dbg_attn[j], attn[j][:], r=[f"attn{j}"])

    P.alias_seed(C_KEYS, B_KEYS)
    P.alias_seed([f"xqfm{j}" for j in range(8)], [f"qfm{j}" for j in range(8)])
    for t in range(8):
        wn = wnat[t % 2]
        wkey = f"wnat{t % 2}"
        E("gpsimd", "dma_start", wn[:], w_out[t * 128:(t + 1) * 128, :], w=[wkey])
        def stw(dc, ps, pk, _t=t):
            E("vector", "tensor_copy", woutT[dc][:, _t * 128:(_t + 1) * 128], ps,
              r=[pk], w=[f"woutT{dc}"])
        transpose8(wn.ap(), [wkey], stw)

    for tt in range(16):
        rb = resid[tt % 2]
        rkey = f"resid{tt % 2}"
        E("gpsimd", "dma_start", rb[:], x_own[tt * 128:(tt + 1) * 128, :], w=[rkey])
        for half in range(2):
            pb = PB6 if half == 0 else PB7
            pk = "PB6" if half == 0 else "PB7"
            for dc in range(8):
                E("tensor", "matmul", pb[:, :], attn[dc][:, tt * 128:(tt + 1) * 128],
                  woutT[dc][:, half * 512:(half + 1) * 512],
                  start=(dc == 0), stop=(dc == 7),
                  r=[f"attn{dc}", f"woutT{dc}"], w=[pk])
            E("vector", "tensor_tensor", y1[tt][:, half * 512:(half + 1) * 512],
              pb[:, :], rb[:, half * 512:(half + 1) * 512], Alu.add,
              r=[pk, rkey], w=[f"y1_{tt}"])
        ln_tile(y1[tt][:], [f"y1_{tt}"], xln2[:], "xln2",
                xc_ap=q_ax[:, 1024:2048], xc_key="q_ax",
                sq_ap=q_t1[:, 1024:2048], sq_key="q_t1")
        fp4_quant(xln2.ap()[:], ["xln2"], D, q_out.ap()[:, 0:D], "q_out", ladder="vector")
        def stq(dc, ps, pk, _tt=tt):
            E("vector", "tensor_copy", xqfm[dc][:, _tt * 128:(_tt + 1) * 128], ps,
              r=[pk], w=[f"xqfm{dc}"])
        transpose8(q_out.ap()[:, 0:D], ["q_out"], stq)

    if DEBUG:
        for t in range(16):
            E("sync", "dma_start", dbg_y1[t], y1[t][:], r=[f"y1_{t}"])
        for j in range(8):
            E("sync", "dma_start", dbg_xq[j], xqfm[j][:], r=[f"xqfm{j}"])

    P.alias_seed(D_KEYS, ["woutT0", "woutT7", "xln2", "resid0", "resid1"])
    P.alias_seed(["hbuf", "hqtm", "hqT0", "hqT1", "silu_t"],
                 [f"attn{j}" for j in range(8)])
    for ic in range(NIC):
        par = ic % 2
        for dc in range(8):
            E("sync", "dma_start", wgqT[par][dc][:], wgscr[dc, :, ic * ICH:(ic + 1) * ICH],
              r=["wgscr"], w=[f"wgqT{par}_{dc}"])
            E("sync", "dma_start", wuqT[par][dc][:], wuscr[dc, :, ic * ICH:(ic + 1) * ICH],
              r=["wuscr"], w=[f"wuqT{par}_{dc}"])
        for ip in range(2):
            E("sync", "dma_start", wdqT[par][ip][:], wdscr[ic * 2 + ip, :, :],
              r=["wdscr"], w=[f"wdqT{par}_{ip}"])
        for tt in range(16):
            for dc in range(8):
                E("tensor", "matmul", PB6[:, 0:ICH], xqfm[dc][:, tt * 128:(tt + 1) * 128],
                  wgqT[par][dc][:], start=(dc == 0), stop=(dc == 7),
                  r=[f"xqfm{dc}", f"wgqT{par}_{dc}"], w=["PB6"])
            for dc in range(8):
                E("tensor", "matmul", PB7[:, 0:ICH], xqfm[dc][:, tt * 128:(tt + 1) * 128],
                  wuqT[par][dc][:], start=(dc == 0), stop=(dc == 7),
                  r=[f"xqfm{dc}", f"wuqT{par}_{dc}"], w=["PB7"])
            if DEBUG and ic == 0 and tt == 0:
                E("vector", "tensor_copy", q_sg[:, 0:ICH], PB6[:, 0:ICH], r=["PB6"], w=["q_sg"])
                E("sync", "dma_start", dbg_gt, q_sg[:, 0:ICH], r=["q_sg"])
            E("scalar", "activation", silu_t[:], PB6[:, 0:ICH], Act.Silu,
              r=["PB6"], w=["silu_t"])
            E("vector", "tensor_tensor", hbuf[:, tt * ICH:(tt + 1) * ICH], silu_t[:],
              PB7[:, 0:ICH], Alu.mult, r=["silu_t", "PB7"], w=["hbuf"])
        for hf in range(2):
            fp4_quant(hbuf.ap()[:, hf * 2048:(hf + 1) * 2048], ["hbuf"], 2048,
                      hqtm.ap()[:, hf * 2048:(hf + 1) * 2048], "hqtm", ladder="vector")
        if DEBUG and ic == 0:
            E("sync", "dma_start", dbg_hb, hbuf[:], r=["hbuf"])
            E("sync", "dma_start", dbg_hq, hqtm[:], r=["hqtm"])
            E("sync", "dma_start", dbg_wg, wgqT[0][0][:], r=["wgqT0_0"])
        for tt in range(16):
            for ip in range(2):
                k = (tt * 2 + ip) % 4
                E("tensor", "matmul", P4[:, k * 128 + 1024:(k + 1) * 128 + 1024],
                  hqtm[:, tt * ICH + ip * 128: tt * ICH + (ip + 1) * 128], s_id[:],
                  start=True, stop=True, r=["hqtm", "id"], w=["P4"])
                E("vector", "tensor_copy", hqT[ip][:, tt * 128:(tt + 1) * 128],
                  P4[:, k * 128 + 1024:(k + 1) * 128 + 1024], r=["P4"], w=[f"hqT{ip}"])
        for tt in range(16):
            for half in range(2):
                pb = PB6 if half == 0 else PB7
                pk = "PB6" if half == 0 else "PB7"
                for ip in range(2):
                    E("tensor", "matmul", pb[:, :], hqT[ip][:, tt * 128:(tt + 1) * 128],
                      wdqT[par][ip][:, half * 512:(half + 1) * 512],
                      start=(ip == 0), stop=(ip == 1),
                      r=[f"hqT{ip}", f"wdqT{par}_{ip}"], w=[pk])
                E("vector", "tensor_tensor", y1[tt][:, half * 512:(half + 1) * 512],
                  y1[tt][:, half * 512:(half + 1) * 512], pb[:, :], Alu.add,
                  r=[pk, f"y1_{tt}"], w=[f"y1_{tt}"])
        if DEBUG and ic < 3:
            E("sync", "dma_start", dbg_y15[ic], y1[15][:], r=["y1_15"])
            if ic == 0:
                E("sync", "dma_start", dbg_y15[3], y1[14][:], r=["y1_14"])
                E("sync", "dma_start", dbg_hqt, hqT[0][:], r=["hqT0"])

    for tt in range(16):
        E("gpsimd", "dma_start", out[tt * 128:(tt + 1) * 128, :], y1[tt][:],
          r=[f"y1_{tt}"])

    P.finalize()
    return nc


_NC = None

def _get_nc():
    global _NC
    if _NC is None:
        _NC = build_program()
    return _NC


def _tables():
    pos = np.arange(L, dtype=np.float64)
    inv = 1.0 / (10000.0 ** (np.arange(32, dtype=np.float64) / 32.0))
    invf = np.tile(inv, 4)
    ang = invf[:, None] * pos[None, :]
    return np.cos(ang).astype(np.float32), np.sin(ang).astype(np.float32)


def kernel(L0_attn_x, L0_attn_w_qkv, L0_attn_w_out, L0_ffn_w_gate, L0_ffn_w_up, L0_ffn_w_down):
    X = np.ascontiguousarray(np.asarray(L0_attn_x, dtype=np.float32))
    wqkv = np.ascontiguousarray(np.asarray(L0_attn_w_qkv, dtype=np.float32))
    wout = np.ascontiguousarray(np.asarray(L0_attn_w_out, dtype=np.float32))
    wg = np.ascontiguousarray(np.asarray(L0_ffn_w_gate, dtype=np.float32))
    wu = np.ascontiguousarray(np.asarray(L0_ffn_w_up, dtype=np.float32))
    wd = np.ascontiguousarray(np.asarray(L0_ffn_w_down, dtype=np.float32))
    cos_f, sin_f = _tables()
    ident = np.eye(128, dtype=np.float32).astype(ml_dtypes.bfloat16)

    nc = _get_nc()
    in_maps = []
    for c in range(8):
        b, hf = c // 2, c % 2
        sl = slice(hf * LQ, (hf + 1) * LQ)
        in_maps.append({
            "x_all": X[b],
            "x_own": np.ascontiguousarray(X[b, sl]),
            "w_qkv": wqkv, "w_out": wout,
            "w_gate": wg, "w_up": wu, "w_down": wd,
            "cos_f": cos_f, "sin_f": sin_f,
            "cos_o": np.ascontiguousarray(cos_f[:, sl]),
            "sin_o": np.ascontiguousarray(sin_f[:, sl]),
            "ident": ident,
        })
    res = run_bass_kernel_spmd(nc, in_maps, core_ids=list(range(8)))
    outv = np.empty((4, L, D), dtype=np.float32)
    for c in range(8):
        b, hf = c // 2, c % 2
        outv[b, hf * LQ:(hf + 1) * LQ] = res.results[c]["out"]
    return outv


# version 06
# speedup vs baseline: 1.2769x; 1.0166x over previous
import numpy as np
import ml_dtypes
import concourse.bass as bass
import concourse.mybir as mybir
from concourse.bass_utils import run_bass_kernel_spmd

F32 = mybir.dt.float32
BF16 = mybir.dt.bfloat16
Alu = mybir.AluOpType
Act = mybir.ActivationFunctionType
AX = mybir.AxisListType

D = 1024
L = 4096
LQ = 2048
H = 16
FI = 2816
ICH = 256
NIC = FI // ICH
VPW = 132


class Buf:
    __slots__ = ("ready", "users")
    def __init__(self):
        self.ready = []
        self.users = []


class Prog:
    ENGINES = ("sync", "gpsimd", "vector", "scalar", "tensor")
    DMA_ENG = {"sync", "gpsimd"}

    def __init__(self, nc):
        self.nc = nc
        self.ops = []
        self.count = {e: 0 for e in self.ENGINES}
        self.B = {}

    def buf(self, key):
        if key not in self.B:
            self.B[key] = Buf()
        return self.B[key]

    def alias_seed(self, new_keys, old_keys):
        toks = []
        for ok in old_keys:
            b = self.buf(ok)
            toks += b.ready + b.users
        for nk in new_keys:
            self.buf(nk).users.extend(toks)

    def emit(self, eng, meth, *args, r=(), w=(), **kw):
        inc = 16 if eng in self.DMA_ENG else 1
        if SERIAL:
            waits = {e: c for e, c in self.count.items() if c > 0 and e != eng}
            self.count[eng] += inc
            tok = (eng, self.count[eng])
            for b in [self.buf(k) for k in w]:
                b.ready = [tok]; b.users = []
            for b in [self.buf(k) for k in r]:
                b.users.append(tok)
            self.ops.append((eng, meth, args, kw, tuple(waits.items()), inc))
            return tok
        waits = {}
        def add(tok, raw=False):
            e, v = tok
            if e != eng:
                waits[e] = max(waits.get(e, 0), v)
            elif raw and eng == "vector" and (self.count[eng] + inc - v) <= 1:
                waits[e] = max(waits.get(e, 0), v)
            elif raw and eng in self.DMA_ENG:
                waits[e] = max(waits.get(e, 0), v)
        reads = [self.buf(k) for k in r]
        writes = [self.buf(k) for k in w]
        for b in reads:
            for t in b.ready:
                add(t, raw=True)
        for b in writes:
            for t in b.ready:
                add(t)
            for t in b.users:
                add(t)
        self.count[eng] += inc
        tok = (eng, self.count[eng])
        for b in writes:
            b.ready = [tok]
            b.users = []
        for b in reads:
            b.users.append(tok)
        self.ops.append((eng, meth, args, kw, tuple(waits.items()), inc))
        return tok

    def finalize(self):
        nc = self.nc
        final = [(e, self.count[e]) for e in self.ENGINES if self.count[e] > 0]
        from contextlib import ExitStack
        stack = ExitStack()
        sems = {e: stack.enter_context(nc.semaphore(f"s_{e}")) for e in self.ENGINES}
        block = stack.enter_context(nc.Block())
        ops = self.ops

        def emit_for(name):
            def emitf(eng):
                done = {}
                for (enm, meth, args, kw, waits, inc) in ops:
                    if enm != name:
                        continue
                    for (we, wv) in waits:
                        if done.get(we, -1) < wv:
                            eng.wait_ge(sems[we], wv)
                            done[we] = wv
                    getattr(eng, meth)(*args, **kw).then_inc(sems[enm], inc)
                for (we, wv) in final:
                    if we != name or name in Prog.DMA_ENG:
                        eng.wait_ge(sems[we], wv)
            return emitf
        block.sync(emit_for("sync"))
        block.gpsimd(emit_for("gpsimd"))
        block.vector(emit_for("vector"))
        block.scalar(emit_for("scalar"))
        block.tensor(emit_for("tensor"))
        stack.close()


DEBUG = False
SERIAL = False

def build_program():
    nc = bass.Bass("TRN2", target_bir_lowering=False)
    P = Prog(nc)
    E = P.emit

    def din(name, shape, dt=F32):
        return nc.dram_tensor(name, shape, dt, kind="ExternalInput").ap()
    x_all = din("x_all", [L, D])
    x_own = din("x_own", [LQ, D])
    w_qkv = din("w_qkv", [3 * D, D])
    w_out = din("w_out", [D, D])
    w_gate = din("w_gate", [FI, D])
    w_up = din("w_up", [FI, D])
    w_down = din("w_down", [D, FI])
    cos_f = din("cos_f", [128, L])
    sin_f = din("sin_f", [128, L])
    cos_o = din("cos_o", [128, LQ])
    sin_o = din("sin_o", [128, LQ])
    ident = din("ident", [128, 128], BF16)
    out = nc.dram_tensor("out", [LQ, D], F32, kind="ExternalOutput").ap()

    scrkind = "ExternalOutput" if DEBUG else "Internal"
    kscr = nc.dram_tensor("kscr", [8, 128, L], BF16, kind=scrkind).ap()
    qscr = nc.dram_tensor("qscr", [H, 64, LQ], BF16, kind=scrkind).ap()
    mscr = nc.dram_tensor("mscr", [H, LQ], BF16, kind=scrkind).ap()
    vscr = nc.dram_tensor("vscr", [8, 32, 128, VPW], BF16, kind=scrkind).ap()
    wgscr = nc.dram_tensor("wgscr", [8, 128, FI], BF16, kind=scrkind).ap()
    wuscr = nc.dram_tensor("wuscr", [8, 128, FI], BF16, kind=scrkind).ap()
    wdscr = nc.dram_tensor("wdscr", [FI // 128, 128, D], BF16, kind=scrkind).ap()
    if DEBUG:
        dbg_xlnf = nc.dram_tensor("dbg_xlnf", [128, 4096], BF16, kind="ExternalOutput").ap()
        dbg_xs = nc.dram_tensor("dbg_xs", [128, D], F32, kind="ExternalOutput").ap()
        dbg_xc = nc.dram_tensor("dbg_xc", [128, D], F32, kind="ExternalOutput").ap()
        dbg_xlnt = nc.dram_tensor("dbg_xlnt", [128, D], BF16, kind="ExternalOutput").ap()
        dbg_id = nc.dram_tensor("dbg_id", [128, 128], BF16, kind="ExternalOutput").ap()
        dbg_lnv = nc.dram_tensor("dbg_lnv", [128, 2], F32, kind="ExternalOutput").ap()
        dbg_kpre = nc.dram_tensor("dbg_kpre", [2, 128, 512], F32, kind="ExternalOutput").ap()
        dbg_q = nc.dram_tensor("dbg_q", [8, 128, LQ], BF16, kind="ExternalOutput").ap()
        dbg_kp = nc.dram_tensor("dbg_kp", [128, L], BF16, kind="ExternalOutput").ap()
        dbg_vp = nc.dram_tensor("dbg_vp", [128, 32 * VPW], BF16, kind="ExternalOutput").ap()
        dbg_pt = nc.dram_tensor("dbg_pt", [128, 2048], BF16, kind="ExternalOutput").ap()
        dbg_av = nc.dram_tensor("dbg_av", [64, 512], F32, kind="ExternalOutput").ap()
        dbg_avd = nc.dram_tensor("dbg_avd", [1, 512], F32, kind="ExternalOutput").ap()
        dbg_st = nc.dram_tensor("dbg_st", [128, 2048], F32, kind="ExternalOutput").ap()
        dbg_hb = nc.dram_tensor("dbg_hb", [128, 16 * ICH], F32, kind="ExternalOutput").ap()
        dbg_hq = nc.dram_tensor("dbg_hq", [128, 16 * ICH], BF16, kind="ExternalOutput").ap()
        dbg_wg = nc.dram_tensor("dbg_wg", [128, ICH], BF16, kind="ExternalOutput").ap()
        dbg_gt = nc.dram_tensor("dbg_gt", [128, ICH], F32, kind="ExternalOutput").ap()
        dbg_y15 = nc.dram_tensor("dbg_y15", [4, 128, D], F32, kind="ExternalOutput").ap()
        dbg_hqt = nc.dram_tensor("dbg_hqt", [128, LQ], BF16, kind="ExternalOutput").ap()
        dbg_attn = nc.dram_tensor("dbg_attn", [8, 128, LQ], BF16, kind="ExternalOutput").ap()
        dbg_bias = nc.dram_tensor("dbg_bias", [1, H], F32, kind="ExternalOutput").ap()
        dbg_y1 = nc.dram_tensor("dbg_y1", [16, 128, D], F32, kind="ExternalOutput").ap()
        dbg_xq = nc.dram_tensor("dbg_xq", [8, 128, LQ], BF16, kind="ExternalOutput").ap()

    def sb(name, shape, dt):
        return nc.alloc_sbuf_tensor(name, shape, dt)

    s_id = sb("s_id", [128, 128], BF16)
    ones2 = sb("ones2", [128, 2], BF16)
    onesr = sb("onesr", [1, 128], F32)
    mbias = sb("mbias", [128, 7], F32)
    mq2 = sb("mq2", [1, H], F32)
    mk2 = sb("mk2", [1, H], F32)
    redt = sb("redt", [1, 1], F32)
    bsc = sb("bsc", [1, H], F32)
    bsc560 = sb("bsc560", [1, H], F32)
    mtmp = sb("mtmp", [1, 512], F32)
    mneg = sb("mneg", [1, 512], BF16)
    ln_m = sb("ln_m", [128, 1], F32)
    ln_v = sb("ln_v", [128, 1], F32)
    ln_r = sb("ln_r", [128, 1], F32)
    wnat = [sb(f"wnat{j}", [128, D], BF16) for j in range(2)]

    base0 = nc.sbuf_base
    base = (base0 + 63) & ~63
    cursor = [base]
    placements = {}

    def place(region, name, shape, dt):
        t = nc.alloc_sbuf_tensor_at(name, shape, dt, offset=placements[region][1])
        sz = int(np.prod(shape[1:])) * mybir.dt.size(dt) if len(shape) > 1 else mybir.dt.size(dt)
        sz = shape[-1] * mybir.dt.size(dt)
        placements[region][1] += (sz + 63) & ~63
        assert placements[region][1] <= placements[region][0] + placements[region][2], \
            f"region {region} overflow at {name}"
        return t

    def region(name, size):
        placements[name] = [cursor[0], cursor[0], size]
        cursor[0] += size

    def reset_region(name):
        placements[name][1] = placements[name][0]

    KB = 1024
    region("R1", 32 * KB)
    region("R2", 34 * KB)
    region("R3", 93 * KB)
    region("R4", 38 * KB)
    assert cursor[0] <= nc.sbuf_top - 64, f"sbuf over: {cursor[0]} > {nc.sbuf_top}"

    qfm = [place("R1", f"qfm{j}", [128, LQ], BF16) for j in range(8)]
    reset_region("R1")
    xqfm = [place("R1", f"xqfm{j}", [128, LQ], BF16) for j in range(8)]
    attn = [place("R2", f"attn{j}", [128, LQ], BF16) for j in range(8)]
    reset_region("R2")
    hbuf = place("R2", "hbuf", [128, 16 * ICH], F32)
    hqtm = place("R2", "hqtm", [128, 16 * ICH], BF16)
    hqT = [place("R2", f"hqT{j}", [128, LQ], BF16) for j in range(2)]
    silu_t = place("R2", "silu_t", [128, ICH], F32)
    wT = [place("R3", f"wT{j}", [128, 3 * D], BF16) for j in range(8)]
    xs = [place("R3", f"xs{j}", [128, D], F32) for j in range(2)]
    xc = place("R3", "xc", [128, D], F32)
    sqt = place("R3", "sqt", [128, D], BF16)
    xlnt = place("R3", "xlnt", [128, D], BF16)
    xlnf = place("R3", "xlnf", [128, 8 * 512], BF16)
    cosw = [place("R3", f"cosw{j}", [128, 512], F32) for j in range(2)]
    sinw = [place("R3", f"sinw{j}", [128, 512], F32) for j in range(2)]
    rt1 = place("R3", "rt1", [128, 512], F32)
    rt2 = place("R3", "rt2", [128, 512], F32)
    kq_pre = place("R3", "kq_pre", [128, 512], BF16)
    sq2 = place("R3", "sq2", [128, 512], BF16)
    vt = place("R3", "vt", [128, 8 * VPW], BF16)
    A_KEYS = [f"wT{j}" for j in range(8)] + \
        ["xs0", "xs1", "xc", "sqt", "xlnt", "xlnf", "cosw0", "cosw1", "sinw0", "sinw1",
         "rt1", "rt2", "kq_pre", "sq2", "vt"]
    reset_region("R3")
    kax = [place("R3", f"kA{j}", [66, L], BF16) for j in range(2)]
    kbx = [place("R3", f"kB{j}", [66, L], BF16) for j in range(2)]
    qx = [place("R3", f"qx{j}", [66, LQ], BF16) for j in range(2)]
    vpair = [place("R3", f"vpair{j}", [128, 32 * VPW], BF16) for j in range(2)]
    pt = [place("R3", f"pt{j}", [128, 2048], BF16) for j in range(2)]
    rdrow = place("R3", "rdrow", [1, 512], F32)
    rdb = place("R3", "rdb", [64, 512], F32)
    wstr = [place("R3", "wstr0", [128, FI], F32)] * 2
    wq_stage = place("R3", "wq_stage", [128, FI], BF16)
    B_KEYS = ["kA0", "kA1", "kB0", "kB1", "qx0", "qx1", "vpair0", "vpair1",
              "ptq0", "ptq1", "ptq2", "ptq3", "rdrow", "rdb", "wstr0", "wq_stage"]
    reset_region("R3")
    y1 = [place("R3", f"y1_{t}", [128, D], F32) for t in range(16)]
    woutT = [place("R3", f"woutT{j}", [128, D], BF16) for j in range(8)]
    xln2 = place("R3", "xln2", [128, D], F32)
    resid = [place("R3", f"resid{j}", [128, D], F32) for j in range(2)]
    C_KEYS = [f"y1_{t}" for t in range(16)] + [f"woutT{j}" for j in range(8)] + \
        ["xln2", "resid0", "resid1"]
    placements["R3"][1] = placements["R3"][0] + 64 * KB
    wgqT = [[place("R3", f"wgqT{p}_{j}", [128, ICH], BF16) for j in range(8)] for p in range(2)]
    wuqT = [[place("R3", f"wuqT{p}_{j}", [128, ICH], BF16) for j in range(8)] for p in range(2)]
    wdqT = [[place("R3", f"wdqT{p}_{j}", [128, D], BF16) for j in range(2)] for p in range(2)]
    D_KEYS = [f"wgqT{p}_{j}" for p in range(2) for j in range(8)] + \
        [f"wuqT{p}_{j}" for p in range(2) for j in range(8)] + \
        [f"wdqT{p}_{j}" for p in range(2) for j in range(2)]
    q_ax = place("R4", "q_ax", [128, 2048], F32)
    q_a = place("R4", "q_a", [128, 2048], F32)
    q_sg = place("R4", "q_sg", [128, 2048], F32)
    q_t0 = place("R4", "q_t0", [128, 2048], BF16)
    q_t1 = place("R4", "q_t1", [128, 2048], BF16)
    q_bm = place("R4", "q_bm", [128, 128], F32)
    q_sc = place("R4", "q_sc", [128, 128], F32)
    q_rs = place("R4", "q_rs", [128, 128], F32)
    q_out = place("R4", "q_out", [128, 2048], BF16)

    P4 = nc.alloc_psum_tensor("P4", [128, 2048], F32)
    PB4 = nc.alloc_psum_tensor("PB4", [128, 512], F32)
    PB5 = nc.alloc_psum_tensor("PB5", [128, 512], F32)
    PB6 = nc.alloc_psum_tensor("PB6", [128, 512], F32)
    PB7 = nc.alloc_psum_tensor("PB7", [128, 512], F32)

    E("sync", "dma_start", s_id[:], ident, w=["id"])
    E("vector", "memset", ones2[:, :], 0.0, w=["ones2"])
    E("vector", "memset", ones2[0:64, 0:1], 1.0, r=["ones2"], w=["ones2"])
    E("vector", "memset", ones2[64:128, 1:2], 1.0, r=["ones2"], w=["ones2"])
    E("vector", "memset", onesr[:, :], 1.0, w=["onesr"])
    for i, m in enumerate([0.25, 0.75, 1.25, 1.75, 2.5, 3.5, 5.0]):
        E("vector", "memset", mbias[:, i:i + 1], -m, w=["mbias"])
    E("vector", "memset", mq2[:, :], 0.0, w=["mq2"])
    E("vector", "memset", mk2[:, :], 0.0, w=["mk2"])

    def transpose8(src_bf_ap, src_keys, store):
        for g in range(2):
            for k in range(4):
                dc = g * 4 + k
                E("tensor", "matmul", P4[:, k * 128:(k + 1) * 128],
                  src_bf_ap[:, dc * 128:(dc + 1) * 128], s_id[:],
                  start=True, stop=True, r=list(src_keys) + ["id"], w=["P4"])
            for k in range(4):
                store(g * 4 + k, P4[:, k * 128:(k + 1) * 128], "P4")

    def build_wT(tiles, dst_col_of):
        for t in tiles:
            wn = wnat[t % 2]
            wkey = f"wnat{t % 2}"
            E("gpsimd", "dma_start", wn[:], w_qkv[t * 128:(t + 1) * 128, :], w=[wkey])
            col = dst_col_of(t)
            def st(dc, ps, pk, _c=col):
                E("vector", "tensor_copy", wT[dc][:, _c:_c + 128], ps,
                  r=[pk], w=[f"wT{dc}"])
            transpose8(wn.ap(), [wkey], st)

    def build_rot():
        for dc in range(8):
            src = wT[dc].ap()
            s3 = src[:, 0:D].rearrange("p (h t) -> p h t", t=64)
            d3 = src[:, D:2 * D].rearrange("p (h t) -> p h t", t=64)
            E("vector", "tensor_scalar", d3[:, :, 0:32], s3[:, :, 32:64], -1.0, None,
              Alu.mult, r=[f"wT{dc}"], w=[f"wT{dc}"])
            E("vector", "tensor_copy", d3[:, :, 32:64], s3[:, :, 0:32],
              r=[f"wT{dc}"], w=[f"wT{dc}"])

    def ln_tile(src_ap, src_keys, dst_ap, dst_key, xc_ap=None, xc_key="xc",
                sq_ap=None, sq_key="sqt"):
        if xc_ap is None:
            xc_ap = xc[:]
        if sq_ap is None:
            sq_ap = sqt[:]
        E("vector", "tensor_reduce", ln_m[:], src_ap, AX.X, Alu.add, r=src_keys, w=["ln_m"])
        E("vector", "tensor_scalar", ln_m[:], ln_m[:], 1.0 / D, None, Alu.mult,
          r=["ln_m"], w=["ln_m"])
        E("vector", "tensor_scalar", xc_ap, src_ap, ln_m[:], None, Alu.subtract,
          r=src_keys + ["ln_m"], w=[xc_key])
        E("scalar", "activation", sq_ap, xc_ap, Act.Square, accum_out=ln_v[:],
          r=[xc_key], w=[sq_key, "ln_v"])
        E("vector", "tensor_scalar", ln_v[:], ln_v[:], 1.0 / D, 1e-5, Alu.mult, Alu.add,
          r=["ln_v"], w=["ln_v"])
        E("scalar", "activation", ln_v[:], ln_v[:], Act.Sqrt, r=["ln_v"], w=["ln_v"])
        E("vector", "reciprocal", ln_r[:], ln_v[:], r=["ln_v"], w=["ln_r"])
        E("vector", "tensor_scalar", dst_ap, xc_ap, ln_r[:], None, Alu.mult,
          r=[xc_key, "ln_r"], w=[dst_key])

    def build_xlnf(x_src, tok0_base):
        for t in range(4):
            tok0 = tok0_base + t * 128
            xsb = xs[t % 2]
            xkey = f"xs{t % 2}"
            E("gpsimd", "dma_start", xsb[:], x_src[tok0:tok0 + 128, :], w=[xkey])
            ln_tile(xsb[:], [xkey], xlnt[:], "xlnt")
            if DEBUG and tok0_base == 0 and t == 0 and x_src is x_all:
                E("sync", "dma_start", dbg_xs, xsb[:], r=[xkey])
                E("sync", "dma_start", dbg_xc, xc[:], r=["xc"])
                E("sync", "dma_start", dbg_xlnt, xlnt[:], r=["xlnt"])
                E("sync", "dma_start", dbg_id, s_id[:], r=["id"])
                E("sync", "dma_start", dbg_lnv[:, 0:1], ln_v[:], r=["ln_v"],
                  allow_slow_non_contiguous=True)
                E("sync", "dma_start", dbg_lnv[:, 1:2], ln_r[:], r=["ln_r"],
                  allow_slow_non_contiguous=True)
            def stx(dc, ps, pk, _t=t):
                E("vector", "tensor_copy",
                  xlnf[:, dc * 512 + _t * 128: dc * 512 + (_t + 1) * 128], ps,
                  r=[pk], w=["xlnf"])
            transpose8(xlnt.ap(), ["xlnt"], stx)

    def gen_rope(fc, cw_ap, sw_ap, cw_keys, out_ap, out_key, m2t, m2key, dbg=None):
        for dc in range(8):
            E("tensor", "matmul", PB4[:, :], wT[dc][:, fc * 128:(fc + 1) * 128],
              xlnf[:, dc * 512:(dc + 1) * 512], start=(dc == 0), stop=(dc == 7),
              r=[f"wT{dc}", "xlnf"], w=["PB4"])
        for dc in range(8):
            E("tensor", "matmul", PB5[:, :], wT[dc][:, D + fc * 128: D + (fc + 1) * 128],
              xlnf[:, dc * 512:(dc + 1) * 512], start=(dc == 0), stop=(dc == 7),
              r=[f"wT{dc}", "xlnf"], w=["PB5"])
        if dbg is not None:
            E("vector", "tensor_copy", rt1[:], PB4[:, :], r=["PB4"], w=["rt1"])
            E("sync", "dma_start", dbg[0], rt1[:], r=["rt1"])
            E("vector", "tensor_copy", rt1[:], PB5[:, :], r=["PB5"], w=["rt1"])
            E("sync", "dma_start", dbg[1], rt1[:], r=["rt1"])
        E("vector", "tensor_tensor", rt1[:], PB4[:, :], cw_ap, Alu.mult,
          r=["PB4"] + cw_keys, w=["rt1"])
        E("vector", "tensor_tensor", rt2[:], PB5[:, :], sw_ap, Alu.mult,
          r=["PB5"] + cw_keys, w=["rt2"])
        E("vector", "tensor_tensor", out_ap, rt1[:], rt2[:], Alu.add,
          r=["rt1", "rt2"], w=[out_key])
        E("vector", "tensor_tensor", sq2[:], out_ap, out_ap, Alu.mult,
          r=[out_key], w=["sq2"])
        for par2 in range(2):
            pnb = PB6 if par2 == 0 else PB7
            pnk = "PB6" if par2 == 0 else "PB7"
            E("tensor", "matmul", pnb[0:1, :], ones2[:, par2:par2 + 1], sq2[:],
              start=True, stop=True, r=["ones2", "sq2"], w=[pnk])
        for par2 in range(2):
            hd = 2 * fc + par2
            pnb = PB6 if par2 == 0 else PB7
            pnk = "PB6" if par2 == 0 else "PB7"
            E("vector", "tensor_reduce", redt[:], pnb[0:1, :], AX.X, Alu.max,
              r=[pnk], w=["redt"])
            E("vector", "tensor_tensor", m2t[0:1, hd:hd + 1], m2t[0:1, hd:hd + 1],
              redt[:], Alu.max, r=["redt", m2key], w=[m2key])

    build_wT(range(8, 16), lambda t: (t - 8) * 128)
    build_wT(range(16, 24), lambda t: 2 * D + (t - 16) * 128)
    build_rot()
    for c in range(8):
        cw0 = c * 512
        par = c % 2
        cwb, swb = cosw[par], sinw[par]
        E("sync", "dma_start", cwb[:], cos_f[:, cw0:cw0 + 512], w=[f"cosw{par}"])
        E("sync", "dma_start", swb[:], sin_f[:, cw0:cw0 + 512], w=[f"sinw{par}"])
        build_xlnf(x_all, cw0)
        if DEBUG and c == 0:
            E("sync", "dma_start", dbg_xlnf, xlnf[:], r=["xlnf"])
        for fc in range(8):
            gen_rope(fc, cwb[:], swb[:], [f"cosw{par}", f"sinw{par}"],
                     kq_pre[:], "kq_pre", mk2, "mk2",
                     dbg=(dbg_kpre if (DEBUG and c == 0 and fc == 0) else None))
            E("sync", "dma_start", kscr[fc, :, cw0:cw0 + 512], kq_pre[:],
              r=["kq_pre"], w=[f"kscr{fc}"])
        for t in range(4):
            for half in range(2):
                pb = PB7 if half == 0 else PB4
                pk = "PB7" if half == 0 else "PB4"
                for dc in range(8):
                    E("tensor", "matmul", pb[:, :],
                      xlnf[:, dc * 512 + t * 128: dc * 512 + (t + 1) * 128],
                      wT[dc][:, 2 * D + half * 512: 2 * D + (half + 1) * 512],
                      start=(dc == 0), stop=(dc == 7),
                      r=[f"wT{dc}", "xlnf"], w=[pk])
            v4 = vt.ap().rearrange("p (h w) -> p h w", w=66)
            E("vector", "memset", v4[:, :, 64:66], 0.0, w=["vt"])
            E("vector", "memset", v4[:, :, 64:65], 1.0, r=["vt"], w=["vt"])
            for half in range(2):
                pb = PB7 if half == 0 else PB4
                pk = "PB7" if half == 0 else "PB4"
                srcv = pb.ap().rearrange("p (hh w) -> p hh w", w=64)
                base_c = 4 * half * VPW
                dstv = vt.ap()[:, base_c: base_c + 4 * VPW].rearrange(
                    "p (hh w) -> p hh w", w=66)[:, :, 0:64]
                E("vector", "tensor_copy", dstv, srcv, r=[pk], w=["vt"])
            for pr in range(8):
                E("gpsimd", "dma_start", vscr[pr, c * 4 + t, :, :],
                  vt[:, pr * VPW:(pr + 1) * VPW], r=["vt"], w=[f"vscr{pr}"])
    build_wT(range(0, 8), lambda t: t * 128)
    build_rot()
    for c in range(4):
        cw0 = c * 512
        par = c % 2
        cwb, swb = cosw[par], sinw[par]
        E("sync", "dma_start", cwb[:], cos_o[:, cw0:cw0 + 512], w=[f"cosw{par}"])
        E("sync", "dma_start", swb[:], sin_o[:, cw0:cw0 + 512], w=[f"sinw{par}"])
        build_xlnf(x_own, cw0)
        for fc in range(8):
            gen_rope(fc, cwb[:], swb[:], [f"cosw{par}", f"sinw{par}"],
                     qfm[fc][:, cw0:cw0 + 512], f"qfm{fc}", mq2, "mq2")
            E("sync", "dma_start", qscr[2 * fc, :, cw0:cw0 + 512],
              qfm[fc][0:64, cw0:cw0 + 512], r=[f"qfm{fc}"], w=["qscr"])
            E("sync", "dma_start", qscr[2 * fc + 1, :, cw0:cw0 + 512],
              qfm[fc][64:128, cw0:cw0 + 512], r=[f"qfm{fc}"], w=["qscr"])

    E("vector", "tensor_tensor", bsc[0:1, :], mq2[0:1, :], mk2[0:1, :], Alu.mult,
      r=["mq2", "mk2"], w=["bsc"])
    E("scalar", "activation", bsc[0:1, :], bsc[0:1, :], Act.Sqrt, r=["bsc"], w=["bsc"])
    E("vector", "tensor_scalar", bsc560[0:1, :], bsc[0:1, :], 1.02, -560.0,
      Alu.mult, Alu.add, r=["bsc"], w=["bsc560"])

    build_wT(range(8, 16), lambda t: (t - 8) * 128)
    build_rot()
    for c in range(4):
        cw0 = c * 512
        par = c % 2
        cwb, swb = cosw[par], sinw[par]
        E("sync", "dma_start", cwb[:], cos_o[:, cw0:cw0 + 512], w=[f"cosw{par}"])
        E("sync", "dma_start", swb[:], sin_o[:, cw0:cw0 + 512], w=[f"sinw{par}"])
        build_xlnf(x_own, cw0)
        for fc in range(8):
            for dc in range(8):
                E("tensor", "matmul", PB4[:, :], wT[dc][:, fc * 128:(fc + 1) * 128],
                  xlnf[:, dc * 512:(dc + 1) * 512], start=(dc == 0), stop=(dc == 7),
                  r=[f"wT{dc}", "xlnf"], w=["PB4"])
            for dc in range(8):
                E("tensor", "matmul", PB5[:, :], wT[dc][:, D + fc * 128: D + (fc + 1) * 128],
                  xlnf[:, dc * 512:(dc + 1) * 512], start=(dc == 0), stop=(dc == 7),
                  r=[f"wT{dc}", "xlnf"], w=["PB5"])
            E("vector", "tensor_tensor", rt1[:], PB4[:, :], cwb[:], Alu.mult,
              r=["PB4", f"cosw{par}"], w=["rt1"])
            E("vector", "tensor_tensor", rt2[:], PB5[:, :], swb[:], Alu.mult,
              r=["PB5", f"sinw{par}"], w=["rt2"])
            E("vector", "tensor_tensor", kq_pre[:], rt1[:], rt2[:], Alu.add,
              r=["rt1", "rt2"], w=["kq_pre"])
            E("vector", "tensor_tensor", sq2[:], qfm[fc][:, cw0:cw0 + 512], kq_pre[:],
              Alu.mult, r=[f"qfm{fc}", "kq_pre"], w=["sq2"])
            for par2 in range(2):
                h = 2 * fc + par2
                E("tensor", "matmul", PB6[0:1, :], ones2[:, par2:par2 + 1], sq2[:],
                  start=True, stop=True, r=["ones2", "sq2"], w=["PB6"])
                E("vector", "tensor_scalar", mtmp[:], PB6[0:1, :],
                  bsc560[0:1, h:h + 1], None, Alu.max, r=["PB6", "bsc560"], w=["mtmp"])
                E("vector", "tensor_scalar", mneg[:], mtmp[:], -1.0, None, Alu.mult,
                  r=["mtmp"], w=["mneg"])
                E("sync", "dma_start", mscr[h, cw0:cw0 + 512], mneg[:],
                  r=["mneg"], w=["mscr"])

    MIDS = [0.25, 0.75, 1.25, 1.75, 2.5, 3.5, 5.0]
    STEP = [0.5, 0.5, 0.5, 0.5, 1.0, 1.0, 2.0]

    def fp4_quant(src_ap, src_keys, n, out_ap, out_key, ladder="vector"):
        assert n <= 2048 and n % 16 == 0
        nb = n // 16
        E("vector", "tensor_scalar", q_sg[:, 0:n], src_ap, 0.0, 2.0, Alu.is_ge, Alu.mult,
          r=src_keys, w=["q_sg"])
        E("vector", "scalar_tensor_tensor", q_ax[:, 0:n], q_sg[:, 0:n], 1.0, src_ap,
          Alu.subtract, Alu.mult, r=["q_sg"] + src_keys, w=["q_ax"])
        ax3 = q_ax.ap()[:, 0:n].rearrange("p (b k) -> p b k", k=16)
        E("vector", "tensor_reduce", q_bm[:, 0:nb], ax3, AX.X, Alu.max,
          r=["q_ax"], w=["q_bm"])
        E("vector", "tensor_scalar", q_sc[:, 0:nb], q_bm[:, 0:nb], 1.0 / 6.0, 1e-12,
          Alu.mult, Alu.max, r=["q_bm"], w=["q_sc"])
        E("vector", "reciprocal", q_rs[:, 0:nb], q_sc[:, 0:nb], r=["q_sc"], w=["q_rs"])
        rs3 = q_rs.ap()[:, 0:nb].unsqueeze(2).broadcast_to([128, nb, 16])
        a3 = q_a.ap()[:, 0:n].rearrange("p (b k) -> p b k", k=16)
        E("vector", "tensor_tensor", a3, ax3, rs3, Alu.mult, r=["q_ax", "q_rs"], w=["q_a"])
        if ladder == "vector":
            E("vector", "tensor_scalar", q_t0[:, 0:n], q_a[:, 0:n], MIDS[0], STEP[0],
              Alu.is_gt, Alu.mult, r=["q_a"], w=["q_t0"])
            for i in range(1, 7):
                E("vector", "tensor_scalar", q_t1[:, 0:n], q_a[:, 0:n], MIDS[i], STEP[i],
                  Alu.is_gt, Alu.mult, r=["q_a"], w=["q_t1"])
                E("vector", "tensor_tensor", q_t0[:, 0:n], q_t0[:, 0:n], q_t1[:, 0:n],
                  Alu.add, r=["q_t1", "q_t0"], w=["q_t0"])
            E("vector", "tensor_scalar", q_out[:, 0:n], q_sg[:, 0:n], 1.0, None,
              Alu.subtract, r=["q_sg"], w=["q_outs"])
            E("vector", "tensor_tensor", q_t1[:, 0:n], q_t0[:, 0:n], q_out[:, 0:n],
              Alu.mult, r=["q_t0", "q_outs"], w=["q_t1"])
        else:
            for i in range(7):
                E("scalar", "activation", q_out[:, 0:n], q_a[:, 0:n], Act.Sign,
                  bias=mbias[:, i:i + 1], r=["q_a", "mbias"], w=["q_outs"])
                if i == 0:
                    E("vector", "tensor_scalar", q_t0[:, 0:n], q_out[:, 0:n],
                      STEP[0] / 2.0, None, Alu.mult, r=["q_outs"], w=["q_t0"])
                else:
                    E("vector", "scalar_tensor_tensor", q_t0[:, 0:n], q_out[:, 0:n],
                      STEP[i] / 2.0, q_t0[:, 0:n], Alu.mult, Alu.add,
                      r=["q_outs", "q_t0"], w=["q_t0"])
            E("vector", "tensor_scalar", q_t0[:, 0:n], q_t0[:, 0:n], 3.0, None,
              Alu.add, r=["q_t0"], w=["q_t0"])
            E("vector", "tensor_scalar", q_out[:, 0:n], q_sg[:, 0:n], 1.0, None,
              Alu.subtract, r=["q_sg"], w=["q_outs"])
            E("vector", "tensor_tensor", q_t1[:, 0:n], q_t0[:, 0:n], q_out[:, 0:n],
              Alu.mult, r=["q_t0", "q_outs"], w=["q_t1"])
        sc3 = q_sc.ap()[:, 0:nb].unsqueeze(2).broadcast_to([128, nb, 16])
        st3 = q_t1.ap()[:, 0:n].rearrange("p (b k) -> p b k", k=16)
        o3 = out_ap.rearrange("p (b k) -> p b k", k=16)
        E("vector", "tensor_tensor", o3, st3, sc3, Alu.mult,
          r=["q_t1", "q_sc"], w=[out_key])

    wtiles = [("g", i) for i in range(FI // 128)] + \
             [("u", i) for i in range(FI // 128)] + \
             [("d", i) for i in range(D // 128)]

    def emit_wtile(kind, ti):
        wn = wstr[0]
        wkey = "wstr0"
        if kind == "g":
            src, n, scr, skey = w_gate[ti * 128:(ti + 1) * 128, :], D, wgscr, "wgscr"
        elif kind == "u":
            src, n, scr, skey = w_up[ti * 128:(ti + 1) * 128, :], D, wuscr, "wuscr"
        else:
            src, n, scr, skey = w_down[ti * 128:(ti + 1) * 128, :], FI, wdscr, "wdscr"
        E("sync", "dma_start", wn[:, 0:n], src, w=[wkey])
        pos = 0
        while pos < n:
            seg = min(2048, n - pos)
            fp4_quant(wn.ap()[:, pos:pos + seg], [wkey], seg,
                      wq_stage.ap()[:, pos:pos + seg], "wq_stage", ladder="vector")
            pos += seg
        nblk = n // 128
        for g in range((nblk + 3) // 4):
            lo, hi = g * 4, min(g * 4 + 4, nblk)
            for k in range(lo, hi):
                E("tensor", "matmul", PB5[:, (k - lo) * 128:(k - lo + 1) * 128],
                  wq_stage[:, k * 128:(k + 1) * 128], s_id[:], start=True, stop=True,
                  r=["wq_stage", "id"], w=["PB5"])
            for k in range(lo, hi):
                dst = scr[k, :, ti * 128:(ti + 1) * 128] if kind == "d" else \
                    scr[k, :, ti * 128:(ti + 1) * 128]
                E("vector", "tensor_copy", q_out[:, (k % 16) * 128:((k % 16) + 1) * 128],
                  PB5[:, (k - lo) * 128:(k - lo + 1) * 128], r=["PB5"], w=["q_out"])
                E("sync", "dma_start", dst,
                  q_out[:, (k % 16) * 128:((k % 16) + 1) * 128], r=["q_out"], w=[skey])

    if DEBUG:
        for j in range(8):
            E("sync", "dma_start", dbg_q[j], qfm[j][:], r=[f"qfm{j}"])
        E("sync", "dma_start", dbg_bias[0:1, :], bsc[0:1, :], r=["bsc"])

    P.alias_seed(B_KEYS, A_KEYS)

    for j in range(2):
        E("vector", "memset", kax[j][64:65, :], 1.0, w=[f"kA{j}"])
        E("vector", "memset", kbx[j][64:65, :], 1.0, w=[f"kB{j}"])

    def load_pair(pr):
        vb = vpair[pr % 2]
        E("sync", "dma_start", kax[pr % 2][0:64, :], kscr[pr, 0:64, :],
          r=[f"kscr{j}" for j in range(8)], w=[f"kA{pr % 2}"])
        E("sync", "dma_start", kbx[pr % 2][0:64, :], kscr[pr, 64:128, :],
          r=[f"kscr{j}" for j in range(8)], w=[f"kB{pr % 2}"])
        for seg in range(4):
            srcv = vscr[pr, seg * 8:(seg + 1) * 8, :, :].transpose([1, 0, 2])
            dstv = vb.ap()[:, seg * 8 * VPW:(seg + 1) * 8 * VPW].rearrange(
                "p (c w) -> p c w", w=VPW)
            E("gpsimd", "dma_start", dstv, srcv,
              r=[f"vscr{pr}"], w=[f"vpair{pr % 2}"])

    def load_q(h):
        qb = qx[h % 2]
        E("sync", "dma_start", qb[0:64, :], qscr[h, :, :], r=["qscr"], w=[f"qx{h % 2}"])
        E("sync", "dma_start", qb[64:65, :], mscr[h:h + 1, :], r=["mscr"], w=[f"qx{h % 2}"])

    NW = len(wtiles)
    wpos = 0
    load_pair(0)
    load_q(0)
    for pr in range(8):
        if pr < 7:
            load_pair(pr + 1)
        vkey = f"vpair{pr % 2}"
        vb = vpair[pr % 2]
        if DEBUG and pr == 0:
            E("sync", "dma_start", dbg_kp[0:64, :], kax[0][0:64, :], r=["kA0"])
            E("sync", "dma_start", dbg_kp[64:128, :], kbx[0][0:64, :], r=["kB0"])
            E("sync", "dma_start", dbg_vp, vb[:], r=[vkey])
        for hh in range(2):
            h = 2 * pr + hh
            if h < H - 1:
                load_q(h + 1)
            off = hh * 64
            kx = kax[pr % 2] if hh == 0 else kbx[pr % 2]
            kkey = (f"kA{pr % 2}" if hh == 0 else f"kB{pr % 2}")
            qb = qx[h % 2]
            qkey = f"qx{h % 2}"
            for qc in range(4):
                q0 = qc * 512

                def emit_qk(kc):
                    s = kc % 4
                    E("tensor", "matmul", P4[:, s * 512:(s + 1) * 512],
                      kx[0:65, kc * 128:(kc + 1) * 128],
                      qb[0:65, q0:q0 + 512],
                      start=True, stop=True, r=[kkey, qkey], w=[f"P4h{s}"])

                def emit_exp(kc):
                    s = kc % 4
                    E("scalar", "activation", pt[0][:, s * 512:(s + 1) * 512],
                      P4[:, s * 512:(s + 1) * 512], Act.Exp, scale=0.125,
                      r=[f"P4h{s}"], w=[f"ptq{s}"])

                def emit_av(kc):
                    s = kc % 4
                    vsl = vb.ap()[:, kc * VPW + hh * 66: kc * VPW + hh * 66 + 65]
                    E("tensor", "matmul", PB4[0:65, :], vsl,
                      pt[0][:, s * 512:(s + 1) * 512],
                      start=(kc == 0), stop=(kc == 31),
                      r=[vkey, f"ptq{s}"], w=["PB4"])

                for kc in range(3):
                    emit_qk(kc)
                    emit_exp(kc)
                for kc in range(3, 32):
                    emit_qk(kc)
                    emit_exp(kc)
                    emit_av(kc - 3)
                for kc in range(29, 32):
                    emit_av(kc)
                if DEBUG and pr == 0 and hh == 0 and qc == 0:
                    E("vector", "tensor_copy", rdb[:], PB4[0:64, :], r=["PB4"], w=["rdb"])
                    E("sync", "dma_start", dbg_av, rdb[:], r=["rdb"])
                    E("vector", "tensor_copy", rdb[0:1, :], PB4[64:65, :], r=["PB4"], w=["rdb"])
                    E("sync", "dma_start", dbg_avd, rdb[0:1, :], r=["rdb"])
                E("vector", "reciprocal", rdrow[:], PB4[64:65, :], r=["PB4"], w=["rdrow"])
                E("tensor", "matmul", PB5[0:64, :], onesr[0:1, 0:64], rdrow[:],
                  start=True, stop=True, r=["onesr", "rdrow"], w=["PB5"])
                E("vector", "tensor_copy", rdb[:], PB5[0:64, :], r=["PB5"], w=["rdb"])
                E("vector", "tensor_tensor", attn[pr][off:off + 64, q0:q0 + 512],
                  PB4[0:64, :], rdb[:], Alu.mult, r=["PB4", "rdb"], w=[f"attn{pr}"])
        wend = (pr + 1) * NW // 8
        while wpos < wend:
            emit_wtile(*wtiles[wpos])
            wpos += 1

    if DEBUG:
        for j in range(8):
            E("sync", "dma_start", dbg_attn[j], attn[j][:], r=[f"attn{j}"])

    P.alias_seed(C_KEYS, B_KEYS)
    P.alias_seed([f"xqfm{j}" for j in range(8)], [f"qfm{j}" for j in range(8)])
    for t in range(8):
        wn = wnat[t % 2]
        wkey = f"wnat{t % 2}"
        E("gpsimd", "dma_start", wn[:], w_out[t * 128:(t + 1) * 128, :], w=[wkey])
        def stw(dc, ps, pk, _t=t):
            E("vector", "tensor_copy", woutT[dc][:, _t * 128:(_t + 1) * 128], ps,
              r=[pk], w=[f"woutT{dc}"])
        transpose8(wn.ap(), [wkey], stw)

    for tt in range(16):
        rb = resid[tt % 2]
        rkey = f"resid{tt % 2}"
        E("gpsimd", "dma_start", rb[:], x_own[tt * 128:(tt + 1) * 128, :], w=[rkey])
        for half in range(2):
            pb = PB6 if half == 0 else PB7
            pk = "PB6" if half == 0 else "PB7"
            for dc in range(8):
                E("tensor", "matmul", pb[:, :], attn[dc][:, tt * 128:(tt + 1) * 128],
                  woutT[dc][:, half * 512:(half + 1) * 512],
                  start=(dc == 0), stop=(dc == 7),
                  r=[f"attn{dc}", f"woutT{dc}"], w=[pk])
            E("vector", "tensor_tensor", y1[tt][:, half * 512:(half + 1) * 512],
              pb[:, :], rb[:, half * 512:(half + 1) * 512], Alu.add,
              r=[pk, rkey], w=[f"y1_{tt}"])
        ln_tile(y1[tt][:], [f"y1_{tt}"], xln2[:], "xln2",
                xc_ap=q_ax[:, 1024:2048], xc_key="q_ax",
                sq_ap=q_t1[:, 1024:2048], sq_key="q_t1")
        fp4_quant(xln2.ap()[:], ["xln2"], D, q_out.ap()[:, 0:D], "q_out", ladder="vector")
        def stq(dc, ps, pk, _tt=tt):
            E("vector", "tensor_copy", xqfm[dc][:, _tt * 128:(_tt + 1) * 128], ps,
              r=[pk], w=[f"xqfm{dc}"])
        transpose8(q_out.ap()[:, 0:D], ["q_out"], stq)

    if DEBUG:
        for t in range(16):
            E("sync", "dma_start", dbg_y1[t], y1[t][:], r=[f"y1_{t}"])
        for j in range(8):
            E("sync", "dma_start", dbg_xq[j], xqfm[j][:], r=[f"xqfm{j}"])

    P.alias_seed(D_KEYS, ["woutT0", "woutT7", "xln2", "resid0", "resid1"])
    P.alias_seed(["hbuf", "hqtm", "hqT0", "hqT1", "silu_t"],
                 [f"attn{j}" for j in range(8)])
    for ic in range(NIC):
        par = ic % 2
        for dc in range(8):
            E("sync", "dma_start", wgqT[par][dc][:], wgscr[dc, :, ic * ICH:(ic + 1) * ICH],
              r=["wgscr"], w=[f"wgqT{par}_{dc}"])
            E("sync", "dma_start", wuqT[par][dc][:], wuscr[dc, :, ic * ICH:(ic + 1) * ICH],
              r=["wuscr"], w=[f"wuqT{par}_{dc}"])
        for ip in range(2):
            E("sync", "dma_start", wdqT[par][ip][:], wdscr[ic * 2 + ip, :, :],
              r=["wdscr"], w=[f"wdqT{par}_{ip}"])
        for tt in range(16):
            for dc in range(8):
                E("tensor", "matmul", PB6[:, 0:ICH], xqfm[dc][:, tt * 128:(tt + 1) * 128],
                  wgqT[par][dc][:], start=(dc == 0), stop=(dc == 7),
                  r=[f"xqfm{dc}", f"wgqT{par}_{dc}"], w=["PB6"])
            for dc in range(8):
                E("tensor", "matmul", PB7[:, 0:ICH], xqfm[dc][:, tt * 128:(tt + 1) * 128],
                  wuqT[par][dc][:], start=(dc == 0), stop=(dc == 7),
                  r=[f"xqfm{dc}", f"wuqT{par}_{dc}"], w=["PB7"])
            if DEBUG and ic == 0 and tt == 0:
                E("vector", "tensor_copy", q_sg[:, 0:ICH], PB6[:, 0:ICH], r=["PB6"], w=["q_sg"])
                E("sync", "dma_start", dbg_gt, q_sg[:, 0:ICH], r=["q_sg"])
            E("scalar", "activation", silu_t[:], PB6[:, 0:ICH], Act.Silu,
              r=["PB6"], w=["silu_t"])
            E("vector", "tensor_tensor", hbuf[:, tt * ICH:(tt + 1) * ICH], silu_t[:],
              PB7[:, 0:ICH], Alu.mult, r=["silu_t", "PB7"], w=["hbuf"])
        for hf in range(2):
            fp4_quant(hbuf.ap()[:, hf * 2048:(hf + 1) * 2048], ["hbuf"], 2048,
                      hqtm.ap()[:, hf * 2048:(hf + 1) * 2048], "hqtm", ladder="vector")
        if DEBUG and ic == 0:
            E("sync", "dma_start", dbg_hb, hbuf[:], r=["hbuf"])
            E("sync", "dma_start", dbg_hq, hqtm[:], r=["hqtm"])
            E("sync", "dma_start", dbg_wg, wgqT[0][0][:], r=["wgqT0_0"])
        for tt in range(16):
            for ip in range(2):
                k = (tt * 2 + ip) % 4
                E("tensor", "matmul", P4[:, k * 128 + 1024:(k + 1) * 128 + 1024],
                  hqtm[:, tt * ICH + ip * 128: tt * ICH + (ip + 1) * 128], s_id[:],
                  start=True, stop=True, r=["hqtm", "id"], w=["P4"])
                E("vector", "tensor_copy", hqT[ip][:, tt * 128:(tt + 1) * 128],
                  P4[:, k * 128 + 1024:(k + 1) * 128 + 1024], r=["P4"], w=[f"hqT{ip}"])
        for tt in range(16):
            for half in range(2):
                pb = PB6 if half == 0 else PB7
                pk = "PB6" if half == 0 else "PB7"
                for ip in range(2):
                    E("tensor", "matmul", pb[:, :], hqT[ip][:, tt * 128:(tt + 1) * 128],
                      wdqT[par][ip][:, half * 512:(half + 1) * 512],
                      start=(ip == 0), stop=(ip == 1),
                      r=[f"hqT{ip}", f"wdqT{par}_{ip}"], w=[pk])
                E("vector", "tensor_tensor", y1[tt][:, half * 512:(half + 1) * 512],
                  y1[tt][:, half * 512:(half + 1) * 512], pb[:, :], Alu.add,
                  r=[pk, f"y1_{tt}"], w=[f"y1_{tt}"])
        if DEBUG and ic < 3:
            E("sync", "dma_start", dbg_y15[ic], y1[15][:], r=["y1_15"])
            if ic == 0:
                E("sync", "dma_start", dbg_y15[3], y1[14][:], r=["y1_14"])
                E("sync", "dma_start", dbg_hqt, hqT[0][:], r=["hqT0"])

    for tt in range(16):
        E("gpsimd", "dma_start", out[tt * 128:(tt + 1) * 128, :], y1[tt][:],
          r=[f"y1_{tt}"])

    P.finalize()
    return nc


_NC = None

def _get_nc():
    global _NC
    if _NC is None:
        _NC = build_program()
    return _NC


def _tables():
    pos = np.arange(L, dtype=np.float64)
    inv = 1.0 / (10000.0 ** (np.arange(32, dtype=np.float64) / 32.0))
    invf = np.tile(inv, 4)
    ang = invf[:, None] * pos[None, :]
    return np.cos(ang).astype(np.float32), np.sin(ang).astype(np.float32)


def kernel(L0_attn_x, L0_attn_w_qkv, L0_attn_w_out, L0_ffn_w_gate, L0_ffn_w_up, L0_ffn_w_down):
    X = np.ascontiguousarray(np.asarray(L0_attn_x, dtype=np.float32))
    wqkv = np.ascontiguousarray(np.asarray(L0_attn_w_qkv, dtype=np.float32))
    wout = np.ascontiguousarray(np.asarray(L0_attn_w_out, dtype=np.float32))
    wg = np.ascontiguousarray(np.asarray(L0_ffn_w_gate, dtype=np.float32))
    wu = np.ascontiguousarray(np.asarray(L0_ffn_w_up, dtype=np.float32))
    wd = np.ascontiguousarray(np.asarray(L0_ffn_w_down, dtype=np.float32))
    cos_f, sin_f = _tables()
    ident = np.eye(128, dtype=np.float32).astype(ml_dtypes.bfloat16)

    nc = _get_nc()
    in_maps = []
    for c in range(8):
        b, hf = c // 2, c % 2
        sl = slice(hf * LQ, (hf + 1) * LQ)
        in_maps.append({
            "x_all": X[b],
            "x_own": np.ascontiguousarray(X[b, sl]),
            "w_qkv": wqkv, "w_out": wout,
            "w_gate": wg, "w_up": wu, "w_down": wd,
            "cos_f": cos_f, "sin_f": sin_f,
            "cos_o": np.ascontiguousarray(cos_f[:, sl]),
            "sin_o": np.ascontiguousarray(sin_f[:, sl]),
            "ident": ident,
        })
    res = run_bass_kernel_spmd(nc, in_maps, core_ids=list(range(8)))
    outv = np.empty((4, L, D), dtype=np.float32)
    for c in range(8):
        b, hf = c // 2, c % 2
        outv[b, hf * LQ:(hf + 1) * LQ] = res.results[c]["out"]
    return outv
